# version 03
# baseline (speedup 1.0000x reference)
import contextlib
import numpy as np
import ml_dtypes
import concourse.bass as bass
import concourse.mybir as mybir
from concourse.bass_utils import run_bass_kernel_spmd

F32 = mybir.dt.float32
BF16 = mybir.dt.bfloat16
I8 = mybir.dt.int8
U32 = mybir.dt.uint32
AF = mybir.ActivationFunctionType
ALU = mybir.AluOpType

D = 2048
T = 2048
BATCH = 4
TC = 512
NCH = T // TC
DBA = 16
DBM = 8
S = 16
RK = 128
NIT = 64
HG = 16
INV_D = 1.0 / D
EPS_LN = 1e-5
PAIRS = [[0, 1], [2, 3], [4, 5], [6, 7]]

ENGINES = ("sync", "pe", "act", "dve", "pool")
OWN_SEM = {"pe": "Pe", "act": "Ac", "dve": "Dv", "pool": "Po"}
def is_dma_sem(s):
    return s[0] in ("L", "G")


class Prog:

    def __init__(self):
        self.ops = []
        self.cnt = {}

    def ev_last_dv(self):
        return ("Dv", self.cnt.get("Dv", 0))

    def add(self, eng, fn, waits=(), inc=None):
        ev = None
        if inc is not None:
            self.cnt[inc] = self.cnt.get(inc, 0) + 1
            ev = (inc, self.cnt[inc])
        self.ops.append((eng, fn, tuple(w for w in waits if w is not None), inc))
        return ev

    def emit_engine(self, name, eng, sems):
        known = {}
        own = OWN_SEM.get(name)
        for oeng, fn, waits, inc in self.ops:
            if oeng != name:
                continue
            for sem, val in waits:
                if sem.endswith("!"):
                    sem = sem[:-1]
                elif sem == own:
                    continue
                tgt = val * 16 if is_dma_sem(sem) else val
                if known.get(sem, 0) < tgt:
                    eng.wait_ge(sems[sem], tgt)
                    known[sem] = tgt
            ins = fn(eng)
            if inc is not None:
                ins.then_inc(sems[inc], 16 if is_dma_sem(inc) else 1)


PART = "all"
SCAN_V_ENG = "pool"


def build_nc():
    nc = bass.Bass("TRN2", target_bir_lowering=False, debug=False)

    xT = nc.dram_tensor("xT", [D, T + 3], F32, kind="ExternalInput").ap()
    cw = nc.dram_tensor("cw", [128, 64], F32, kind="ExternalInput").ap()
    cbi = nc.dram_tensor("cbi", [128, 16], F32, kind="ExternalInput").ap()
    dpi = nc.dram_tensor("dpi", [128, 8], F32, kind="ExternalInput").ap()
    dtbi = nc.dram_tensor("dtbi", [128, 8], F32, kind="ExternalInput").ap()
    alog = nc.dram_tensor("alog", [128, 128], F32, kind="ExternalInput").ap()
    wxp = nc.dram_tensor("wxp", [D, 160], F32, kind="ExternalInput").ap()
    wdt = nc.dram_tensor("wdt", [RK, 1024], F32, kind="ExternalInput").ap()
    esel = nc.dram_tensor("esel", [32, 2048], F32, kind="ExternalInput").ap()
    esel2 = nc.dram_tensor("esel2", [32, 2048], F32, kind="ExternalInput").ap()
    onescol = nc.dram_tensor("onescol", [1, 128], F32, kind="ExternalInput").ap()
    ones128 = nc.dram_tensor("ones128", [128, 1], F32, kind="ExternalInput").ap()
    hgu = nc.dram_tensor("hgu", [NIT, 128, 4096], I8, kind="ExternalInput").ap()
    hd = nc.dram_tensor("hd", [NIT, 128, 2048], I8, kind="ExternalInput").ap()
    sgd = nc.dram_tensor("sgd", [128, 64], F32, kind="ExternalInput").ap()
    sud = nc.dram_tensor("sud", [128, 64], F32, kind="ExternalInput").ap()
    sdd = nc.dram_tensor("sdd", [128, 16], F32, kind="ExternalInput").ap()
    offs = nc.dram_tensor("offs", [1, 32], U32, kind="ExternalInput").ap()

    out = nc.dram_tensor("out", [D, 1024], F32, kind="ExternalOutput").ap()

    ybin = [nc.dram_tensor(f"ybin{t}", [1024, TC], F32) for t in range(NCH)]
    ybag = nc.dram_tensor("ybag", [NCH * D, TC], F32)
    fuse_d = nc.dram_tensor("fuse_d", [NIT, 128, TC], BF16)

    ctx = contextlib.ExitStack()
    sb = lambda name, shape, dt: ctx.enter_context(nc.sbuf_tensor(name, shape, dt))
    psb = lambda name, shape: ctx.enter_context(nc.psum_tensor(name, shape, F32))

    with ctx:
        cw_s = sb("cw_s", [128, 64], F32)
        cb_s = sb("cb_s", [128, 16], F32)
        dp_s = sb("dp_s", [128, 8], F32)
        dtb_s = sb("dtb_s", [128, 8], F32)
        A_s = sb("A_s", [128, 128], F32)
        Atmp_s = sb("Atmp_s", [128, 128], F32)
        flsh_s = sb("flsh_s", [128, 128], F32)
        wxp_s = sb("wxp_s", [128, 16 * 160], BF16)
        wdt_s = sb("wdt_s", [128, 1024], BF16)
        esel_s = sb("esel_s", [32, 2048], F32)
        esel2_s = sb("esel2_s", [32, 2048], F32)
        onec_s = sb("onec_s", [1, 128], F32)
        one128_s = sb("one128_s", [128, 1], F32)
        sg_s = sb("sg_s", [128, 64], F32)
        su_s = sb("su_s", [128, 64], F32)
        sd_s = sb("sd_s", [128, 16], F32)
        offs_s = sb("offs_s", [1, 32], U32)
        state = sb("state", [128, 128], F32)
        bc_s = sb("bc_s", [32, TC], F32)
        dtr_s = sb("dtr_s", [128, TC], BF16)

        AB = 146 * 1024
        arena = sb("arena", [128, AB], I8)
        a32 = arena.bitcast(F32)
        a16 = arena.bitcast(BF16)

        class Alloc:
            def __init__(self):
                self.cur = 0

            def f32(self, nelem):
                off = (self.cur + 3) // 4 * 4
                self.cur = off + nelem * 4
                assert self.cur <= AB, f"arena overflow {self.cur}"
                return off // 4

            def bf16(self, nelem):
                off = (self.cur + 1) // 2 * 2
                self.cur = off + nelem * 2
                assert self.cur <= AB, f"arena overflow {self.cur}"
                return off // 2

        M = Alloc()
        o_xt = [M.f32(TC + 3) for _ in range(4)]
        o_acc = [M.f32(TC) for _ in range(2)]
        o_xcm = M.f32(DBM * TC)
        o_xcb = M.bf16(DBA * TC)
        o_dt = M.f32(DBM * TC)
        o_w = M.f32(DBM * TC)
        o_y = M.f32(DBM * TC)
        o_dA = [M.f32(TC) for _ in range(4)]
        o_u = [M.f32(TC) for _ in range(4)]
        o_h = [M.f32(TC) for _ in range(4)]
        o_v = [M.f32(TC) for _ in range(4)]
        o_csb = [M.f32(TC) for _ in range(2)]
        o_sp0 = [M.f32(TC) for _ in range(2)]
        o_sp1 = [M.f32(TC) for _ in range(2)]
        o_rl = [M.f32(TC) for _ in range(2)]
        o_usb = [M.f32(TC) for _ in range(2)]

        F = Alloc()
        o_yf = F.f32(HG * TC)
        o_y2 = F.f32(HG * TC)
        o_x1b = F.bf16(HG * TC)
        o_wgu = [F.bf16(4096) for _ in range(2)]
        o_wd = [F.bf16(1024) for _ in range(2)]
        o_frd = [F.bf16(TC) for _ in range(2)]
        o_gt = [F.f32(TC) for _ in range(2)]
        o_ft = [F.bf16(TC) for _ in range(2)]
        o_sq = [F.f32(TC) for _ in range(2)]
        o_rows = F.f32(4 * TC)
        o_ost = [F.f32(TC) for _ in range(2)]

        f32v = lambda off, n, p0=0, p1=128: a32[p0:p1, off:off + n]
        b16v = lambda off, n, p0=0, p1=128: a16[p0:p1, off:off + n]

        xtv = lambda i: f32v(o_xt[i], TC + 3)
        xts = lambda i, a, n: f32v(o_xt[i] + a, n)
        accv = lambda i: f32v(o_acc[i], TC)
        xcm = lambda db: f32v(o_xcm + db * TC, TC)
        xcb = lambda db: b16v(o_xcb + db * TC, TC)
        dtv = lambda db: f32v(o_dt + db * TC, TC)
        wv_ = lambda db: f32v(o_w + db * TC, TC)
        yv = lambda db: f32v(o_y + db * TC, TC)
        dAv = lambda i: f32v(o_dA[i], TC)
        uv = lambda i: f32v(o_u[i], TC)
        hv = lambda i: f32v(o_h[i], TC)
        vv = lambda i: f32v(o_v[i], TC)
        csbv = lambda i: f32v(o_csb[i], TC)
        sp0v = lambda i: f32v(o_sp0[i], TC)
        sp1v = lambda i: f32v(o_sp1[i], TC)
        rlv = lambda i: f32v(o_rl[i], TC)
        usbv = lambda i: f32v(o_usb[i], TC)

        yfv = lambda hg: f32v(o_yf + hg * TC, TC)
        y2v = lambda hg: f32v(o_y2 + hg * TC, TC)
        x1bv = lambda hg: b16v(o_x1b + hg * TC, TC)
        wguv = lambda i, a, n: b16v(o_wgu[i] + a, n)
        wdv = lambda i, a, n: b16v(o_wd[i] + a, n)
        frdv = lambda i: b16v(o_frd[i], TC)
        gtv = lambda i: f32v(o_gt[i], TC)
        ftv = lambda i: b16v(o_ft[i], TC)
        sqv = lambda i: f32v(o_sq[i], TC)
        row = lambda i: a32[0:1, o_rows + i * TC: o_rows + (i + 1) * TC]
        ostv = lambda i: f32v(o_ost[i], TC)

        pb = [psb(f"pb{i}", [128, 512]) for i in range(8)]

        sem_names = (["Lc", "Gc", "Gy", "Lf0", "Lf1", "Cc", "Pe", "Ac", "Dv", "Po"]
                     + [f"Lx{i}" for i in range(4)] + [f"Ly{i}" for i in range(NCH)]
                     + [f"Lr{i}" for i in range(2)] + [f"Lo{i}" for i in range(2)]
                     + [f"Gw{i}" for i in range(2)] + [f"Gv{i}" for i in range(2)])
        sems = {n: ctx.enter_context(nc.semaphore(n)) for n in sem_names}

        P = Prog()

        const_loads = [("cw", cw_s, cw), ("cb", cb_s, cbi), ("dp", dp_s, dpi),
                       ("dtb", dtb_s, dtbi), ("alog", Atmp_s, alog),
                       ("esel", esel_s, esel), ("esel2", esel2_s, esel2), ("onec", onec_s, onescol),
                       ("one128", one128_s, ones128), ("sg", sg_s, sgd),
                       ("su", su_s, sud), ("sd", sd_s, sdd), ("offs", offs_s, offs)]
        const_evs = {}
        for name, dst, src in const_loads:
            const_evs[name] = P.add("sync", lambda e, d=dst, s=src: e.dma_start(out=d[:], in_=s), inc="Lc")
        ev_wxp = P.add("pool", lambda e: e.dma_start(
            out=wxp_s.rearrange("p (kg f) -> p kg f", kg=16),
            in_=wxp.rearrange("(kg p) f -> p kg f", p=128)), inc="Gc")
        ev_wdt = P.add("pool", lambda e: e.dma_start(out=wdt_s[:], in_=wdt), inc="Gc")
        ev_flush = P.add("sync", lambda e: e.dma_start(out=flsh_s[:], in_=alog), inc="Lc")
        evA1 = P.add("act", lambda e: e.activation(A_s[:], Atmp_s[:], AF.Exp),
                     waits=[ev_flush, const_evs["alog"]], inc="Ac")
        evA = P.add("dve", lambda e: e.tensor_scalar_mul(A_s[:], A_s[:], -1.0),
                    waits=[evA1], inc="Dv")

        xt_war = [None] * 4
        acc_war = [None] * 2
        dA_war = [None] * 4
        h_war_pool = [None] * 4
        h_war_act = [None] * 4
        csb_war = [None] * 2
        sp_war = [None] * 2
        psB_war = [None] * 2
        psC_war = [None] * 2
        psdt_war = [None] * 2
        ag_evs = []
        dt_ready = [None] * DBM
        y_init_ev = [None] * DBM
        chunk_bar = []
        prev_cb2 = None
        prev_evC_last = None

        for tau in range(NCH):
            t0 = tau * TC
            xc_ready = [None] * DBA
            for db in range(DBA):
                it4 = db % 4
                ld_ev = P.add("sync", lambda e, i=it4, r0=db * 128, c0=t0: e.dma_start(
                    out=xtv(i), in_=xT[r0:r0 + 128, c0:c0 + TC + 3]),
                    waits=[xt_war[it4]] + (chunk_bar if db < 4 else []), inc=f"Lx{it4}")
                i2 = db % 2
                tp = P.add("dve", lambda e, i2=i2, i=it4, db=db: e.tensor_scalar(
                    accv(i2), xts(i, 0, TC), cw_s[:, db * 4:db * 4 + 1], None, ALU.mult),
                    waits=[ld_ev, acc_war[i2], evA] + (chunk_bar if db < 2 else []),
                    inc="Dv")
                for j in range(1, 3):
                    tp = P.add("dve", lambda e, i2=i2, i=it4, db=db, j=j: e.scalar_tensor_tensor(
                        accv(i2), xts(i, j, TC), cw_s[:, db * 4 + j:db * 4 + j + 1],
                        accv(i2), ALU.mult, ALU.add),
                        waits=[("Dv!", tp[1])], inc="Dv")
                tap_ev = P.add("dve", lambda e, i2=i2, i=it4, db=db: e.scalar_tensor_tensor(
                    accv(i2), xts(i, 3, TC), cw_s[:, db * 4 + 3:db * 4 + 4],
                    accv(i2), ALU.mult, ALU.add), waits=[("Dv!", tp[1])], inc="Dv")
                if PART == "xtd" and tau == 1 and db < 4:
                    xd_ev = P.add("sync", lambda e, i=it4, db=db: e.dma_start(
                        out=out[db * 128:(db + 1) * 128, 0:TC + 3], in_=xtv(i)),
                        waits=[tap_ev], inc=f"Ly{tau % NCH}")
                    tap_ev = xd_ev
                if db < DBM:
                    sl_ev = P.add("act", lambda e, i2=i2, db=db: e.activation(
                        xcm(db), accv(i2), AF.Silu, bias=cb_s[:, db:db + 1]),
                        waits=[tap_ev] + (chunk_bar if db < 2 else []), inc="Ac")
                    cp_ev = P.add("dve", lambda e, db=db: e.tensor_copy(xcb(db), xcm(db)),
                                  waits=[sl_ev], inc="Dv")
                    yi_ev = P.add("dve", lambda e, db=db, i=it4: e.scalar_tensor_tensor(
                        yv(db), xcm(db), dp_s[:, db:db + 1], xts(i, 3, TC),
                        ALU.mult, ALU.add), inc="Dv")
                    y_init_ev[db] = yi_ev
                    xc_ready[db] = cp_ev
                    xt_war[it4] = yi_ev
                    acc_war[i2] = cp_ev
                else:
                    sl_ev = P.add("act", lambda e, i2=i2, db=db: e.activation(
                        xcb(db), accv(i2), AF.Silu, bias=cb_s[:, db:db + 1]),
                        waits=[tap_ev], inc="Ac")
                    xc_ready[db] = sl_ev
                    xt_war[it4] = tap_ev
                    acc_war[i2] = sl_ev
            if PART == "conv":
                chunk_bar = [xc_ready[DBA - 1], y_init_ev[DBM - 1]]
                continue
            proj_ev = None
            for kg in range(DBA):
                st, sp = kg == 0, kg == DBA - 1
                P.add("pe", lambda e, kg=kg, st=st, sp=sp: e.matmul(
                    pb[0][:, :], wxp_s[:, kg * 160:kg * 160 + 128], xcb(kg), start=st, stop=sp),
                    waits=[xc_ready[kg], ev_wdt] + (chunk_bar if st else []),
                    inc="Pe" if sp else None)
                proj_ev = P.add("pe", lambda e, kg=kg, st=st, sp=sp: e.matmul(
                    pb[1][0:32, :], wxp_s[:, kg * 160 + 128:kg * 160 + 160], xcb(kg),
                    start=st, stop=sp), inc="Pe" if sp else None) or proj_ev
            ev_dtr = P.add("act", lambda e: e.copy(dtr_s[:], pb[0][:, :]),
                           waits=[proj_ev], inc="Ac")
            ev_bc = P.add("act", lambda e: e.copy(bc_s[:], pb[1][0:32, :]),
                          waits=[proj_ev, prev_cb2, prev_evC_last], inc="Ac")
            if PART == "proj":
                chunk_bar = [proj_ev, ev_dtr, ev_bc, y_init_ev[DBM - 1]]
                continue
            for db in range(DBM):
                i2 = db % 2
                mm_ev = P.add("pe", lambda e, db=db, i2=i2: e.matmul(
                    pb[2 + i2][:, :], wdt_s[:, db * 128:(db + 1) * 128], dtr_s[:],
                    start=True, stop=True),
                    waits=[ev_dtr, ev_wdt, psdt_war[i2]], inc="Pe")
                if PART == "dtmm":
                    dt_ev = P.add("act", lambda e, db=db, i2=i2: e.copy(
                        dtv(db), pb[2 + i2][:, :]), waits=[mm_ev, sp_war[i2]], inc="Ac")
                    P.add("dve", lambda e, db=db: e.tensor_mul(wv_(db), dtv(db), xcm(db)),
                          waits=[dt_ev], inc="Dv")
                    dt_ready[db] = dt_ev
                    sp_war[i2] = dt_ev
                    psdt_war[i2] = dt_ev
                    continue
                if PART == "dtrl":
                    rl_ev = P.add("dve", lambda e, db=db, i2=i2: e.tensor_scalar(
                        rlv(i2), pb[2 + i2][:, :], dtb_s[:, db:db + 1], 0.0,
                        ALU.add, ALU.max), waits=[mm_ev, sp_war[i2]])
                    dt_ev = P.add("dve", lambda e, db=db, i2=i2: e.tensor_copy(
                        dtv(db), rlv(i2)), inc="Dv")
                    P.add("dve", lambda e, db=db: e.tensor_mul(wv_(db), dtv(db), xcm(db)),
                          inc="Dv")
                    dt_ready[db] = dt_ev
                    sp_war[i2] = dt_ev
                    psdt_war[i2] = dt_ev
                    continue
                if PART == "dtact":
                    P.add("act", lambda e, db=db, i2=i2: e.activation(
                        sp0v(i2), pb[2 + i2][:, :], AF.Abs, bias=dtb_s[:, db:db + 1]),
                        waits=[mm_ev, sp_war[i2]])
                    P.add("act", lambda e, i2=i2: e.activation(sp1v(i2), sp0v(i2), AF.Exp, scale=-1.0))
                    ab_ev = P.add("act", lambda e, i2=i2: e.activation(
                        sp0v(i2), sp1v(i2), AF.Ln, bias=1.0), inc="Ac")
                    dt_ev = P.add("dve", lambda e, db=db, i2=i2: e.tensor_copy(
                        dtv(db), sp0v(i2)), waits=[ab_ev], inc="Dv")
                    P.add("dve", lambda e, db=db: e.tensor_mul(wv_(db), dtv(db), xcm(db)),
                          inc="Dv")
                    dt_ready[db] = dt_ev
                    sp_war[i2] = dt_ev
                    psdt_war[i2] = dt_ev
                    continue
                ev_usb = P.add("dve", lambda e, db=db, i2=i2: e.tensor_scalar_add(
                    usbv(i2), pb[2 + i2][:, :], dtb_s[:, db:db + 1]),
                    waits=[mm_ev, sp_war[i2]], inc="Dv")
                P.add("dve", lambda e, i2=i2: e.tensor_scalar_max(
                    rlv(i2), usbv(i2), 0.0))
                P.add("act", lambda e, i2=i2: e.activation(sp0v(i2), usbv(i2), AF.Abs),
                      waits=[ev_usb, sp_war[i2]])
                P.add("act", lambda e, i2=i2: e.activation(sp1v(i2), sp0v(i2), AF.Exp, scale=-1.0))
                ab_ev = P.add("act", lambda e, i2=i2: e.activation(
                    sp0v(i2), sp1v(i2), AF.Ln, bias=1.0), inc="Ac")
                dt_ev = P.add("dve", lambda e, db=db, i2=i2: e.tensor_add(
                    dtv(db), rlv(i2), sp0v(i2)), waits=[ab_ev], inc="Dv")
                P.add("dve", lambda e, db=db: e.tensor_mul(wv_(db), dtv(db), xcm(db)), inc="Dv")
                dt_ready[db] = dt_ev
                sp_war[i2] = dt_ev
                psdt_war[i2] = ev_usb
            if PART in ("dtw", "prep") and tau < 2:
                P.add("sync", lambda e, tau=tau: e.dma_start(
                    out=out[0:1024, tau * TC:(tau + 1) * TC].rearrange("(db p) t -> p db t", p=128),
                    in_=a32[:, o_xcm:o_xcm + DBM * TC].rearrange("p (db t) -> p db t", db=DBM)),
                    waits=[dt_ready[DBM - 1], P.ev_last_dv()], inc=f"Ly{tau}")
                P.add("pool", lambda e, tau=tau: e.dma_start(
                    out=out[1024:2048, tau * TC:(tau + 1) * TC].rearrange("(db p) t -> p db t", p=128),
                    in_=a16[:, o_xcb:o_xcb + DBM * TC].rearrange("p (db t) -> p db t", db=DBM)),
                    waits=[dt_ready[DBM - 1]], inc="Gy")
            if PART in ("prep", "dtmm", "dtrl", "dtact", "xtd"):
                chunk_bar = [proj_ev, ev_dtr, ev_bc, dt_ready[DBM - 1], y_init_ev[DBM - 1]]
                continue
            scan_ev = None
            last_evx = None
            yadd_last = None
            for s in range(S):
                s2 = s % 2
                evB = P.add("pe", lambda e, s=s, s2=s2: e.matmul(
                    pb[4 + s2][:, :], esel_s[:, s * 128:(s + 1) * 128], bc_s[:],
                    start=True, stop=True),
                    waits=[ev_bc, psB_war[s2], ev_flush], inc="Pe")
                evC = P.add("pe", lambda e, s=s, s2=s2: e.matmul(
                    pb[6 + s2][:, :], esel2_s[:, s * 128:(s + 1) * 128], bc_s[:],
                    start=True, stop=True), waits=[ev_bc, ev_flush, psC_war[s2]], inc="Pe")
                prev_evC_last = evC
                evCe = P.add("act", lambda e, s2=s2: e.copy(csbv(s2), pb[6 + s2][:, :]),
                             waits=[evC, csb_war[s2]], inc="Ac")
                psC_war[s2] = evCe
                for db in range(DBM):
                    i = s * 8 + db
                    ii = i % 4
                    evdA = P.add("act", lambda e, db=db, ii=ii, col=db * 16 + s: e.activation(
                        dAv(ii), dtv(db), AF.Exp, scale=A_s[:, col:col + 1]),
                        waits=[dt_ready[db], dA_war[ii]], inc="Ac")
                    P.add("dve", lambda e, db=db, ii=ii, s2=s2: e.tensor_mul(
                        uv(ii), wv_(db), pb[4 + s2][:, :]), waits=[evB])
                    P.add("dve", lambda e, ii=ii: e.tensor_scalar_max(uv(ii), uv(ii), 0.0))
                    init = 0.0 if tau == 0 else state[:, i:i + 1]
                    evh = P.add("dve", lambda e, ii=ii, init=init: e.tensor_tensor_scan(
                        hv(ii), dAv(ii), uv(ii), init, ALU.mult, ALU.add),
                        waits=[evdA, h_war_pool[ii], h_war_act[ii]], inc="Dv")
                    scan_ev = evh
                    evx = P.add("act", lambda e, ii=ii, i=i: e.copy(
                        state[:, i:i + 1], hv(ii)[:, TC - 1:TC]), waits=[evh], inc="Ac")
                    last_evx = evx
                    P.add(SCAN_V_ENG, lambda e, ii=ii, s2=s2: e.tensor_mul(
                        vv(ii), hv(ii), csbv(s2)),
                        waits=[evh, evCe] + ([y_init_ev[db]] if s == 0 else []))
                    evy = P.add(SCAN_V_ENG, lambda e, ii=ii, db=db: e.tensor_add(
                        yv(db), yv(db), vv(ii)), inc="Po" if SCAN_V_ENG == "pool" else "Dv")
                    yadd_last = evy
                    dA_war[ii] = evh
                    h_war_pool[ii] = evy
                    h_war_act[ii] = evx
                psB_war[s2] = scan_ev
                csb_war[s2] = yadd_last
            yb_ev = P.add("sync", lambda e, tau=tau: e.dma_start(
                out=ybin[tau].ap().rearrange("(db p) t -> p db t", p=128),
                in_=a32[:, o_y:o_y + DBM * TC].rearrange("p (db t) -> p db t", db=DBM)),
                waits=[yadd_last], inc=f"Ly{tau}")
            ag_ev = P.add("pool", lambda e, tau=tau: e.collective_compute(
                "AllGather", ALU.bypass, replica_groups=PAIRS,
                ins=[ybin[tau].ap().opt()],
                outs=[ybag.ap()[tau * D:(tau + 1) * D, :].opt()]),
                waits=[yb_ev], inc="Cc")
            ag_evs.append(ag_ev)
            prev_cb2 = None
            chunk_bar = [scan_ev, yadd_last, yb_ev, proj_ev, ev_dtr, ev_bc, last_evx]

        if PART == "dtw":
            prev = ag_evs[-1]
            P.add("sync", lambda e: e.nop(), waits=[prev])
        if PART == "mixer":
            prev = ag_evs[-1]
            for q in range(2 * HG):
                eq = P.add("pool", lambda e, q=q: e.dma_start(
                    out=f32v(o_yf + (q % HG) * TC, TC),
                    in_=ybag.ap()[q * 128:(q + 1) * 128, :]),
                    waits=[prev], inc="Gy")
                prev = P.add("sync", lambda e, q=q: e.dma_start(
                    out=out[(q % HG) * 128:((q % HG) + 1) * 128, (q // HG) * TC:(q // HG + 1) * TC],
                    in_=f32v(o_yf + (q % HG) * TC, TC)), waits=[eq], inc="Lo0")

        all_ag = ag_evs[-1] if ag_evs else None
        slab_bar = []
        for sl in (range(2) if PART == "all" else ()):
            def yf_load(e, k=sl * 16):
                v = e.value_load(offs_s[0:1, k:k + 1])
                src = ybag.ap()[bass.ds(v, D), :].rearrange("(hg p) t -> p hg t", p=128)
                return e.dma_start(
                    out=a32[:, o_yf:o_yf + HG * TC].rearrange("p (hg t) -> p hg t", hg=HG),
                    in_=src)
            ev_yf = P.add("pool", yf_load,
                          waits=[all_ag, ev_flush] + slab_bar, inc="Gy")
            yf_evs = [ev_yf] * HG

            def emit_ln_stats(src_v, pre_evs, extra0):
                sw = [None, None]
                ev2 = None
                for hg in range(HG):
                    evq = P.add("act", lambda e, hg=hg: e.activation(
                        sqv(hg % 2), src_v(hg), AF.Square),
                        waits=[pre_evs[hg], sw[hg % 2]] + (extra0 if hg < 2 else []), inc="Ac")
                    P.add("pe", lambda e, hg=hg: e.matmul(
                        pb[0][0:1, :], one128_s[:], src_v(hg),
                        start=(hg == 0), stop=(hg == HG - 1)),
                        waits=[pre_evs[hg], ev_flush] + (extra0 if hg == 0 else []))
                    ev2 = P.add("pe", lambda e, hg=hg: e.matmul(
                        pb[1][0:1, :], one128_s[:], sqv(hg % 2),
                        start=(hg == 0), stop=(hg == HG - 1)),
                        waits=[evq], inc="Pe")
                    sw[hg % 2] = ev2
                return ev2, ev2

            def emit_ln_rows(stat_ev):
                P.add("dve", lambda e: e.tensor_scalar_mul(row(0), pb[0][0:1, :], INV_D),
                      waits=[stat_ev])
                P.add("dve", lambda e: e.tensor_scalar_mul(row(3), pb[1][0:1, :], INV_D))
                P.add("dve", lambda e: e.tensor_mul(row(2), row(0), row(0)))
                P.add("dve", lambda e: e.tensor_sub(row(3), row(3), row(2)))
                ev_var = P.add("dve", lambda e: e.tensor_scalar_add(row(3), row(3), EPS_LN),
                               inc="Dv")
                ev_ln = P.add("act", lambda e: e.activation(row(2), row(3), AF.Ln),
                              waits=[ev_var], inc="Ac")
                ev_hf = P.add("dve", lambda e: e.tensor_scalar_mul(row(2), row(2), -0.5),
                              waits=[ev_ln], inc="Dv")
                ev_rstd = P.add("act", lambda e: e.activation(row(1), row(2), AF.Exp),
                                waits=[ev_hf], inc="Ac")
                ev_b = P.add("dve", lambda e: e.scalar_tensor_tensor(
                    row(2), row(0), -1.0, row(1), ALU.mult, ALU.mult),
                    waits=[ev_rstd], inc="Dv")
                eva_r = P.add("pe", lambda e: e.matmul(pb[2][:, :], onec_s[:], row(1),
                                                       start=True, stop=True),
                              waits=[ev_b, ev_flush], inc="Pe")
                evb_r = P.add("pe", lambda e: e.matmul(pb[3][:, :], onec_s[:], row(2),
                                                       start=True, stop=True), inc="Pe")
                return ev_b, eva_r, evb_r

            st1, st2 = emit_ln_stats(yfv, yf_evs, slab_bar)
            ev_b, eva_r, evb_r = emit_ln_rows(st2)
            x1_evs = []
            for hg in range(HG):
                P.add("dve", lambda e, hg=hg: e.tensor_mul(yfv(hg), yfv(hg), pb[2][:, :]),
                      waits=[evb_r])
                P.add("dve", lambda e, hg=hg: e.tensor_add(yfv(hg), yfv(hg), pb[3][:, :]))
                x1_evs.append(P.add("dve", lambda e, hg=hg: e.tensor_copy(x1bv(hg), yfv(hg)),
                                    inc="Dv"))
            wgu_war = [None, None]
            gu_done = [None, None]
            ft_war = [None, None]
            gt_war = [None, None]
            fdma_evs = []
            for it in range(NIT):
                ii = it % 2
                evw = P.add("pool", lambda e, it=it, ii=ii: e.dma_start(
                    out=b16v(o_wgu[ii], 4096), in_=hgu[it]), waits=[wgu_war[ii]], inc=f"Gw{ii}")
                ba, bb = 4 + 2 * ii, 5 + 2 * ii
                evg = evu = None
                for kg in range(DBA):
                    st, sp = kg == 0, kg == DBA - 1
                    evg = P.add("pe", lambda e, kg=kg, st=st, sp=sp, ba=ba, ii=ii: e.matmul(
                        pb[ba][:, :], wguv(ii, kg * 128, 128), x1bv(kg), start=st, stop=sp),
                        waits=[evw, x1_evs[kg]] + ([gu_done[ii]] if st else []),
                        inc="Pe" if sp else None) or evg
                    evu = P.add("pe", lambda e, kg=kg, st=st, sp=sp, bb=bb, ii=ii: e.matmul(
                        pb[bb][:, :], wguv(ii, 2048 + kg * 128, 128), x1bv(kg),
                        start=st, stop=sp), inc="Pe" if sp else None) or evu
                wgu_war[ii] = evu
                evgt = P.add("act", lambda e, it=it, ii=ii, ba=ba: e.activation(
                    gtv(ii), pb[ba][:, :], AF.Silu, scale=sg_s[:, it:it + 1]),
                    waits=[evg, evu, gt_war[ii]], inc="Ac")
                evft = P.add("dve", lambda e, it=it, ii=ii, bb=bb: e.scalar_tensor_tensor(
                    ftv(ii), pb[bb][:, :], su_s[:, it:it + 1], gtv(ii), ALU.mult, ALU.mult),
                    waits=[evgt, ft_war[ii]], inc="Dv")
                gu_done[ii] = evft
                gt_war[ii] = evft
                evfd = P.add("sync", lambda e, it=it, ii=ii: e.dma_start(
                    out=fuse_d[it], in_=ftv(ii)), waits=[evft], inc=f"Lf{ii}")
                fdma_evs.append(evfd)
                ft_war[ii] = evfd
            ev_evac_all = [None] * HG
            prev_evac = []
            wd_war = [None, None]
            frd_war = [None, None]
            for hh in range(2):
                down_done = [None] * 8
                evd = None
                for it in range(NIT):
                    ii = it % 2
                    evwd = P.add("pool", lambda e, it=it, ii=ii, hh=hh: e.dma_start(
                        out=b16v(o_wd[ii], 1024), in_=hd[it, :, hh * 1024:(hh + 1) * 1024]),
                        waits=[wd_war[ii]], inc=f"Gv{ii}")
                    evfr = P.add("sync", lambda e, it=it, ii=ii: e.dma_start(
                        out=frdv(ii), in_=fuse_d[it]),
                        waits=[fdma_evs[-2], fdma_evs[-1], frd_war[ii]], inc=f"Lr{ii}")
                    for g8 in range(8):
                        st, sp = it == 0, it == NIT - 1
                        w0 = [evwd, evfr]
                        if st:
                            w0 = w0 + [ev_b, x1_evs[-1]] + prev_evac
                            if hh == 0:
                                w0 = w0 + [gu_done[0], gu_done[1]]
                        evd = P.add("pe", lambda e, ii=ii, g8=g8, st=st, sp=sp: e.matmul(
                            pb[g8][:, :], wdv(ii, g8 * 128, 128), frdv(ii),
                            start=st, stop=sp), waits=w0,
                            inc="Pe" if (g8 == 7 or sp) else None) or evd
                        if sp:
                            down_done[g8] = evd
                    wd_war[ii] = evd
                    frd_war[ii] = evd
                cur = []
                for g8 in range(8):
                    hg = hh * 8 + g8
                    eve = P.add("dve", lambda e, g8=g8, hg=hg: e.scalar_tensor_tensor(
                        y2v(hg), pb[g8][:, :], sd_s[:, hg:hg + 1], yfv(hg),
                        ALU.mult, ALU.add), waits=[down_done[g8]], inc="Dv")
                    cur.append(eve)
                    ev_evac_all[hg] = eve
                prev_evac = cur
            st1, st2 = emit_ln_stats(y2v, ev_evac_all, ev_evac_all)
            ev_b2, eva_r2, evb_r2 = emit_ln_rows(st2)
            ost_war = [None, None]
            slab_bar = []
            for hg in range(HG):
                P.add("dve", lambda e, hg=hg: e.tensor_mul(ostv(hg % 2), y2v(hg), pb[2][:, :]),
                      waits=[evb_r2, ost_war[hg % 2]])
                evo = P.add("dve", lambda e, hg=hg: e.tensor_add(
                    ostv(hg % 2), ostv(hg % 2), pb[3][:, :]), inc="Dv")
                evod = P.add("sync", lambda e, hg=hg, sl=sl: e.dma_start(
                    out=out[hg * 128:(hg + 1) * 128, sl * TC:(sl + 1) * TC], in_=ostv(hg % 2)),
                    waits=[evo], inc=f"Lo{hg % 2}")
                ost_war[hg % 2] = evod
                slab_bar.append(evod)
            slab_bar.append(ev_evac_all[-1])
            slab_bar.append(st2)

        P.add("sync", lambda e: e.nop(),
              waits=[(s, P.cnt[s]) for s in list(P.cnt) if is_dma_sem(s)])

        with nc.Block() as block:
            @block.sync
            def _(eng):
                P.emit_engine("sync", eng, sems)

            @block.tensor
            def _(eng):
                P.emit_engine("pe", eng, sems)

            @block.scalar
            def _(eng):
                P.emit_engine("act", eng, sems)

            @block.vector
            def _(eng):
                P.emit_engine("dve", eng, sems)

            @block.gpsimd
            def _(eng):
                P.emit_engine("pool", eng, sems)

    return nc


_NC_CACHE = {}


def _get_nc():
    if "nc" not in _NC_CACHE:
        _NC_CACHE["nc"] = build_nc()
    return _NC_CACHE["nc"]


def kernel(L0_attn_x, L0_attn_conv_w, L0_attn_conv_b, L0_attn_x_proj_w,
           L0_attn_dt_proj_w, L0_attn_dt_proj_b, L0_attn_A_log, L0_attn_D_param,
           L0_ffn_w_gate, L0_ffn_w_up, L0_ffn_w_down, L0_ffn_s_gate,
           L0_ffn_s_up, L0_ffn_s_down):
    x = np.asarray(L0_attn_x, np.float32)
    conv_w = np.asarray(L0_attn_conv_w, np.float32)
    conv_b = np.asarray(L0_attn_conv_b, np.float32)
    x_proj_w = np.asarray(L0_attn_x_proj_w, np.float32)
    dt_proj_w = np.asarray(L0_attn_dt_proj_w, np.float32)
    dt_proj_b = np.asarray(L0_attn_dt_proj_b, np.float32)
    A_log = np.asarray(L0_attn_A_log, np.float32)
    D_param = np.asarray(L0_attn_D_param, np.float32)
    wg8 = np.asarray(L0_ffn_w_gate)
    wu8 = np.asarray(L0_ffn_w_up)
    wd8 = np.asarray(L0_ffn_w_down)
    s_gate = np.asarray(L0_ffn_s_gate, np.float32)
    s_up = np.asarray(L0_ffn_s_up, np.float32)
    s_down = np.asarray(L0_ffn_s_down, np.float32)

    def half_prep(role):
        lo = role * 1024
        perm = np.concatenate([np.arange(lo, lo + 1024),
                               np.arange((1 - role) * 1024, (1 - role) * 1024 + 1024)])
        d = {}
        d["cw"] = np.ascontiguousarray(
            conv_w[perm, 0, :].reshape(16, 128, 4).transpose(1, 0, 2).reshape(128, 64))
        d["cbi"] = np.ascontiguousarray(conv_b[perm].reshape(16, 128).T)
        d["dpi"] = np.ascontiguousarray(D_param[lo:lo + 1024].reshape(8, 128).T)
        d["dtbi"] = np.ascontiguousarray(dt_proj_b[lo:lo + 1024].reshape(8, 128).T)
        d["alog"] = np.ascontiguousarray(
            A_log[lo:lo + 1024].reshape(8, 128, 16).transpose(1, 0, 2).reshape(128, 128))
        d["wxp"] = np.ascontiguousarray(x_proj_w.T[perm])
        d["wdt"] = np.ascontiguousarray(dt_proj_w[lo:lo + 1024].T)
        off = np.zeros((1, 32), np.uint32)
        for sl in range(2):
            for hg in range(HG):
                off[0, sl * 16 + hg] = (role * 2 + sl) * D + hg * 128
        d["offs"] = off
        return d, perm

    halves = [half_prep(0), half_prep(1)]

    esel = np.zeros((32, 2048), np.float32)
    esel2 = np.zeros((32, 2048), np.float32)
    for s in range(16):
        esel[s, s * 128:(s + 1) * 128] = 1.0
        esel2[16 + s, s * 128:(s + 1) * 128] = 1.0
    onescol = np.ones((1, 128), np.float32)
    ones128 = np.ones((128, 1), np.float32)
    g4 = wg8.reshape(16, 128, 64, 128).transpose(2, 1, 0, 3)
    u4 = wu8.reshape(16, 128, 64, 128).transpose(2, 1, 0, 3)
    hgu = np.ascontiguousarray(np.stack([g4, u4], axis=2).reshape(NIT, 128, 4096))
    hd = np.ascontiguousarray(wd8.reshape(NIT, 128, 2048))
    sgl = np.ascontiguousarray(s_gate.reshape(64, 128).T)
    sul = np.ascontiguousarray(s_up.reshape(64, 128).T)
    sdl = np.ascontiguousarray(s_down.reshape(16, 128).T)

    in_maps = []
    for c in range(8):
        b, role = c // 2, c % 2
        d, perm = halves[role]
        xTh = np.zeros((D, T + 3), np.float32)
        xTh[:, 3:] = x[b].T[perm]
        m = {"xT": xTh, "esel": esel, "esel2": esel2, "onescol": onescol, "ones128": ones128,
             "hgu": hgu, "hd": hd, "sgd": sgl, "sud": sul, "sdd": sdl}
        m.update(d)
        in_maps.append(m)

    nc = _get_nc()
    res = run_bass_kernel_spmd(nc, in_maps, core_ids=list(range(8)))

    result = np.empty((BATCH, T, D), np.float32)
    for c in range(8):
        b, role = c // 2, c % 2
        o = np.asarray(res.results[c]["out"])
        result[b, role * 1024:(role + 1) * 1024, :] = o.T
    return result




# version 04
# speedup vs baseline: 1.0056x; 1.0056x over previous
import contextlib
import numpy as np
import ml_dtypes
import concourse.bass as bass
import concourse.mybir as mybir
from concourse.bass_utils import run_bass_kernel_spmd

F32 = mybir.dt.float32
BF16 = mybir.dt.bfloat16
I8 = mybir.dt.int8
U32 = mybir.dt.uint32
AF = mybir.ActivationFunctionType
ALU = mybir.AluOpType

D = 2048
T = 2048
BATCH = 4
TC = 512
NCH = T // TC
DBA = 16
DBM = 8
S = 16
RK = 128
NIT = 64
HG = 16
INV_D = 1.0 / D
EPS_LN = 1e-5
PAIRS = [[0, 1], [2, 3], [4, 5], [6, 7]]

ENGINES = ("sync", "pe", "act", "dve", "pool")
OWN_SEM = {"pe": "Pe", "act": "Ac", "dve": "Dv", "pool": "Po"}
def is_dma_sem(s):
    return s[0] in ("L", "G")


class Prog:

    def __init__(self):
        self.ops = []
        self.cnt = {}

    def ev_last_dv(self):
        return ("Dv", self.cnt.get("Dv", 0))

    def add(self, eng, fn, waits=(), inc=None):
        ev = None
        if inc is not None:
            self.cnt[inc] = self.cnt.get(inc, 0) + 1
            ev = (inc, self.cnt[inc])
        self.ops.append((eng, fn, tuple(w for w in waits if w is not None), inc))
        return ev

    def emit_engine(self, name, eng, sems):
        known = {}
        own = OWN_SEM.get(name)
        for oeng, fn, waits, inc in self.ops:
            if oeng != name:
                continue
            for sem, val in waits:
                if sem.endswith("!"):
                    sem = sem[:-1]
                elif sem == own:
                    continue
                tgt = val * 16 if is_dma_sem(sem) else val
                if known.get(sem, 0) < tgt:
                    eng.wait_ge(sems[sem], tgt)
                    known[sem] = tgt
            ins = fn(eng)
            if inc is not None:
                ins.then_inc(sems[inc], 16 if is_dma_sem(inc) else 1)


PART = "all"
SCAN_V_ENG = "pool"


def build_nc():
    nc = bass.Bass("TRN2", target_bir_lowering=False, debug=False)

    xT = nc.dram_tensor("xT", [D, T + 3], F32, kind="ExternalInput").ap()
    cw = nc.dram_tensor("cw", [128, 64], F32, kind="ExternalInput").ap()
    cbi = nc.dram_tensor("cbi", [128, 16], F32, kind="ExternalInput").ap()
    dpi = nc.dram_tensor("dpi", [128, 8], F32, kind="ExternalInput").ap()
    dtbi = nc.dram_tensor("dtbi", [128, 8], F32, kind="ExternalInput").ap()
    alog = nc.dram_tensor("alog", [128, 128], F32, kind="ExternalInput").ap()
    wxp = nc.dram_tensor("wxp", [D, 160], F32, kind="ExternalInput").ap()
    wdt = nc.dram_tensor("wdt", [RK, 1024], F32, kind="ExternalInput").ap()
    esel = nc.dram_tensor("esel", [32, 2048], F32, kind="ExternalInput").ap()
    esel2 = nc.dram_tensor("esel2", [32, 2048], F32, kind="ExternalInput").ap()
    onescol = nc.dram_tensor("onescol", [1, 128], F32, kind="ExternalInput").ap()
    ones128 = nc.dram_tensor("ones128", [128, 1], F32, kind="ExternalInput").ap()
    hgu = nc.dram_tensor("hgu", [NIT, 128, 4096], I8, kind="ExternalInput").ap()
    hd = nc.dram_tensor("hd", [NIT, 128, 2048], I8, kind="ExternalInput").ap()
    sgd = nc.dram_tensor("sgd", [128, 64], F32, kind="ExternalInput").ap()
    sud = nc.dram_tensor("sud", [128, 64], F32, kind="ExternalInput").ap()
    sdd = nc.dram_tensor("sdd", [128, 16], F32, kind="ExternalInput").ap()
    offs = nc.dram_tensor("offs", [1, 32], U32, kind="ExternalInput").ap()

    out = nc.dram_tensor("out", [D, 1024], F32, kind="ExternalOutput").ap()

    ybin = [nc.dram_tensor(f"ybin{t}", [1024, TC], F32) for t in range(NCH)]
    ybag = nc.dram_tensor("ybag", [NCH * D, TC], F32)
    fuse_d = nc.dram_tensor("fuse_d", [NIT, 128, TC], BF16)

    ctx = contextlib.ExitStack()
    sb = lambda name, shape, dt: ctx.enter_context(nc.sbuf_tensor(name, shape, dt))
    psb = lambda name, shape: ctx.enter_context(nc.psum_tensor(name, shape, F32))

    with ctx:
        cw_s = sb("cw_s", [128, 64], F32)
        cb_s = sb("cb_s", [128, 16], F32)
        dp_s = sb("dp_s", [128, 8], F32)
        dtb_s = sb("dtb_s", [128, 8], F32)
        A_s = sb("A_s", [128, 128], F32)
        Atmp_s = sb("Atmp_s", [128, 128], F32)
        flsh_s = sb("flsh_s", [128, 128], F32)
        wxp_s = sb("wxp_s", [128, 16 * 160], BF16)
        wdt_s = sb("wdt_s", [128, 1024], BF16)
        esel_s = sb("esel_s", [32, 2048], F32)
        esel2_s = sb("esel2_s", [32, 2048], F32)
        onec_s = sb("onec_s", [1, 128], F32)
        one128_s = sb("one128_s", [128, 1], F32)
        sg_s = sb("sg_s", [128, 64], F32)
        su_s = sb("su_s", [128, 64], F32)
        sd_s = sb("sd_s", [128, 16], F32)
        offs_s = sb("offs_s", [1, 32], U32)
        state = sb("state", [128, 128], F32)
        bc_s = sb("bc_s", [32, TC], F32)
        dtr_s = sb("dtr_s", [128, TC], BF16)

        AB = 158 * 1024
        arena = sb("arena", [128, AB], I8)
        a32 = arena.bitcast(F32)
        a16 = arena.bitcast(BF16)

        class Alloc:
            def __init__(self):
                self.cur = 0

            def f32(self, nelem):
                off = (self.cur + 3) // 4 * 4
                self.cur = off + nelem * 4
                assert self.cur <= AB, f"arena overflow {self.cur}"
                return off // 4

            def bf16(self, nelem):
                off = (self.cur + 1) // 2 * 2
                self.cur = off + nelem * 2
                assert self.cur <= AB, f"arena overflow {self.cur}"
                return off // 2

        M = Alloc()
        o_xt = [M.f32(TC + 3) for _ in range(4)]
        o_acc = [M.f32(TC) for _ in range(2)]
        o_acc2 = [M.f32(TC) for _ in range(2)]
        o_xcm = M.f32(DBM * TC)
        o_xcb = M.bf16(DBA * TC)
        o_dt = M.f32(DBM * TC)
        o_w = M.f32(DBM * TC)
        o_y = M.f32(DBM * TC)
        o_dA = [M.f32(TC) for _ in range(4)]
        o_dbx = [M.f32(TC) for _ in range(4)]
        o_u = [M.f32(TC) for _ in range(4)]
        o_h = [M.f32(TC) for _ in range(4)]
        o_v = [M.f32(TC) for _ in range(4)]
        o_csb = [M.f32(TC) for _ in range(2)]
        o_sp0 = [M.f32(TC) for _ in range(2)]
        o_sp1 = [M.f32(TC) for _ in range(2)]
        o_rl = [M.f32(TC) for _ in range(2)]
        o_usb = [M.f32(TC) for _ in range(2)]

        F = Alloc()
        o_yf = F.f32(HG * TC)
        o_y2 = F.f32(HG * TC)
        o_x1b = F.bf16(HG * TC)
        o_wgu = [F.bf16(4096) for _ in range(2)]
        o_wd = [F.bf16(1024) for _ in range(2)]
        o_frd = [F.bf16(TC) for _ in range(2)]
        o_gt = [F.f32(TC) for _ in range(2)]
        o_ft = [F.bf16(TC) for _ in range(2)]
        o_sq = [F.f32(TC) for _ in range(2)]
        o_rows = F.f32(4 * TC)
        o_ost = [F.f32(TC) for _ in range(2)]

        f32v = lambda off, n, p0=0, p1=128: a32[p0:p1, off:off + n]
        b16v = lambda off, n, p0=0, p1=128: a16[p0:p1, off:off + n]

        xtv = lambda i: f32v(o_xt[i], TC + 3)
        xts = lambda i, a, n: f32v(o_xt[i] + a, n)
        accv = lambda i: f32v(o_acc[i], TC)
        acc2v = lambda i: f32v(o_acc2[i], TC)
        xcm = lambda db: f32v(o_xcm + db * TC, TC)
        xcb = lambda db: b16v(o_xcb + db * TC, TC)
        dtv = lambda db: f32v(o_dt + db * TC, TC)
        wv_ = lambda db: f32v(o_w + db * TC, TC)
        yv = lambda db: f32v(o_y + db * TC, TC)
        dAv = lambda i: f32v(o_dA[i], TC)
        dbxv = lambda i: f32v(o_dbx[i], TC)
        uv = lambda i: f32v(o_u[i], TC)
        hv = lambda i: f32v(o_h[i], TC)
        vv = lambda i: f32v(o_v[i], TC)
        csbv = lambda i: f32v(o_csb[i], TC)
        sp0v = lambda i: f32v(o_sp0[i], TC)
        sp1v = lambda i: f32v(o_sp1[i], TC)
        rlv = lambda i: f32v(o_rl[i], TC)
        usbv = lambda i: f32v(o_usb[i], TC)

        yfv = lambda hg: f32v(o_yf + hg * TC, TC)
        y2v = lambda hg: f32v(o_y2 + hg * TC, TC)
        x1bv = lambda hg: b16v(o_x1b + hg * TC, TC)
        wguv = lambda i, a, n: b16v(o_wgu[i] + a, n)
        wdv = lambda i, a, n: b16v(o_wd[i] + a, n)
        frdv = lambda i: b16v(o_frd[i], TC)
        gtv = lambda i: f32v(o_gt[i], TC)
        ftv = lambda i: b16v(o_ft[i], TC)
        sqv = lambda i: f32v(o_sq[i], TC)
        row = lambda i: a32[0:1, o_rows + i * TC: o_rows + (i + 1) * TC]
        ostv = lambda i: f32v(o_ost[i], TC)

        pb = [psb(f"pb{i}", [128, 512]) for i in range(8)]

        sem_names = (["Lc", "Gc", "Gy", "Lf0", "Lf1", "Cc", "Pe", "Ac", "Dv", "Po"]
                     + [f"Lx{i}" for i in range(4)] + [f"Ly{i}" for i in range(NCH)]
                     + [f"Lr{i}" for i in range(2)] + [f"Lo{i}" for i in range(2)]
                     + [f"Gw{i}" for i in range(2)] + [f"Gv{i}" for i in range(2)])
        sems = {n: ctx.enter_context(nc.semaphore(n)) for n in sem_names}

        P = Prog()

        const_loads = [("cw", cw_s, cw), ("cb", cb_s, cbi), ("dp", dp_s, dpi),
                       ("dtb", dtb_s, dtbi), ("alog", Atmp_s, alog),
                       ("esel", esel_s, esel), ("esel2", esel2_s, esel2), ("onec", onec_s, onescol),
                       ("one128", one128_s, ones128), ("sg", sg_s, sgd),
                       ("su", su_s, sud), ("sd", sd_s, sdd), ("offs", offs_s, offs)]
        const_evs = {}
        for name, dst, src in const_loads:
            const_evs[name] = P.add("sync", lambda e, d=dst, s=src: e.dma_start(out=d[:], in_=s), inc="Lc")
        ev_wxp = P.add("pool", lambda e: e.dma_start(
            out=wxp_s.rearrange("p (kg f) -> p kg f", kg=16),
            in_=wxp.rearrange("(kg p) f -> p kg f", p=128)), inc="Gc")
        ev_wdt = P.add("pool", lambda e: e.dma_start(out=wdt_s[:], in_=wdt), inc="Gc")
        ev_flush = P.add("sync", lambda e: e.dma_start(out=flsh_s[:], in_=alog), inc="Lc")
        evA1 = P.add("act", lambda e: e.activation(A_s[:], Atmp_s[:], AF.Exp),
                     waits=[ev_flush, const_evs["alog"]], inc="Ac")
        evA = P.add("dve", lambda e: e.tensor_scalar_mul(A_s[:], A_s[:], -1.0),
                    waits=[evA1], inc="Dv")

        xt_war = [None] * 4
        acc_war = [None] * 2
        dA_war = [None] * 4
        h_war_pool = [None] * 4
        h_war_act = [None] * 4
        v_war = [None] * 4
        csb_war = [None] * 2
        sp_war = [None] * 2
        psB_war = [None] * 2
        psC_war = [None] * 2
        psdt_war = [None] * 2
        ag_evs = []
        dt_ready = [None] * DBM
        y_init_ev = [None] * DBM
        chunk_bar = []
        prev_cb2 = None
        prev_evC_last = None

        for tau in range(NCH):
            t0 = tau * TC
            xc_ready = [None] * DBA
            for db in range(DBA):
                it4 = db % 4
                ld_ev = P.add("sync", lambda e, i=it4, r0=db * 128, c0=t0: e.dma_start(
                    out=xtv(i), in_=xT[r0:r0 + 128, c0:c0 + TC + 3]),
                    waits=[xt_war[it4]] + (chunk_bar if db < 4 else []), inc=f"Lx{it4}")
                i2 = db % 2
                P.add("dve", lambda e, i2=i2, i=it4, db=db: e.tensor_scalar(
                    accv(i2), xts(i, 0, TC), cw_s[:, db * 4:db * 4 + 1], None, ALU.mult),
                    waits=[ld_ev, acc_war[i2], evA] + (chunk_bar if db < 2 else []))
                P.add("dve", lambda e, i2=i2, i=it4, db=db: e.scalar_tensor_tensor(
                    acc2v(i2), xts(i, 1, TC), cw_s[:, db * 4 + 1:db * 4 + 2],
                    accv(i2), ALU.mult, ALU.add))
                P.add("dve", lambda e, i2=i2, i=it4, db=db: e.scalar_tensor_tensor(
                    accv(i2), xts(i, 2, TC), cw_s[:, db * 4 + 2:db * 4 + 3],
                    acc2v(i2), ALU.mult, ALU.add))
                tap_ev = P.add("dve", lambda e, i2=i2, i=it4, db=db: e.scalar_tensor_tensor(
                    acc2v(i2), xts(i, 3, TC), cw_s[:, db * 4 + 3:db * 4 + 4],
                    accv(i2), ALU.mult, ALU.add), inc="Dv")
                if PART == "xtd" and tau == 1 and db < 4:
                    xd_ev = P.add("sync", lambda e, i=it4, db=db: e.dma_start(
                        out=out[db * 128:(db + 1) * 128, 0:TC + 3], in_=xtv(i)),
                        waits=[tap_ev], inc=f"Ly{tau % NCH}")
                    tap_ev = xd_ev
                if db < DBM:
                    sl_ev = P.add("act", lambda e, i2=i2, db=db: e.activation(
                        xcm(db), acc2v(i2), AF.Silu, bias=cb_s[:, db:db + 1]),
                        waits=[tap_ev] + (chunk_bar if db < 2 else []), inc="Ac")
                    cp_ev = P.add("dve", lambda e, db=db: e.tensor_copy(xcb(db), xcm(db)),
                                  waits=[sl_ev], inc="Dv")
                    yi_ev = P.add("dve", lambda e, db=db, i=it4: e.scalar_tensor_tensor(
                        yv(db), xcm(db), dp_s[:, db:db + 1], xts(i, 3, TC),
                        ALU.mult, ALU.add), inc="Dv")
                    y_init_ev[db] = yi_ev
                    xc_ready[db] = cp_ev
                    xt_war[it4] = yi_ev
                    acc_war[i2] = cp_ev
                else:
                    sl_ev = P.add("act", lambda e, i2=i2, db=db: e.activation(
                        xcb(db), acc2v(i2), AF.Silu, bias=cb_s[:, db:db + 1]),
                        waits=[tap_ev], inc="Ac")
                    xc_ready[db] = sl_ev
                    xt_war[it4] = tap_ev
                    acc_war[i2] = sl_ev
            if PART == "conv":
                chunk_bar = [xc_ready[DBA - 1], y_init_ev[DBM - 1]]
                continue
            proj_ev = None
            for kg in range(DBA):
                st, sp = kg == 0, kg == DBA - 1
                P.add("pe", lambda e, kg=kg, st=st, sp=sp: e.matmul(
                    pb[0][:, :], wxp_s[:, kg * 160:kg * 160 + 128], xcb(kg), start=st, stop=sp),
                    waits=[xc_ready[kg], ev_wdt] + (chunk_bar if st else []),
                    inc="Pe" if sp else None)
                proj_ev = P.add("pe", lambda e, kg=kg, st=st, sp=sp: e.matmul(
                    pb[1][0:32, :], wxp_s[:, kg * 160 + 128:kg * 160 + 160], xcb(kg),
                    start=st, stop=sp), inc="Pe" if sp else None) or proj_ev
            ev_dtr = P.add("act", lambda e: e.copy(dtr_s[:], pb[0][:, :]),
                           waits=[proj_ev], inc="Ac")
            ev_bc = P.add("act", lambda e: e.copy(bc_s[:], pb[1][0:32, :]),
                          waits=[proj_ev, prev_cb2, prev_evC_last], inc="Ac")
            if PART == "proj":
                chunk_bar = [proj_ev, ev_dtr, ev_bc, y_init_ev[DBM - 1]]
                continue
            for db in range(DBM):
                i2 = db % 2
                mm_ev = P.add("pe", lambda e, db=db, i2=i2: e.matmul(
                    pb[2 + i2][:, :], wdt_s[:, db * 128:(db + 1) * 128], dtr_s[:],
                    start=True, stop=True),
                    waits=[ev_dtr, ev_wdt, psdt_war[i2]], inc="Pe")
                if PART == "dtmm":
                    dt_ev = P.add("act", lambda e, db=db, i2=i2: e.copy(
                        dtv(db), pb[2 + i2][:, :]), waits=[mm_ev, sp_war[i2]], inc="Ac")
                    P.add("dve", lambda e, db=db: e.tensor_mul(wv_(db), dtv(db), xcm(db)),
                          waits=[dt_ev], inc="Dv")
                    dt_ready[db] = dt_ev
                    sp_war[i2] = dt_ev
                    psdt_war[i2] = dt_ev
                    continue
                if PART == "dtrl":
                    rl_ev = P.add("dve", lambda e, db=db, i2=i2: e.tensor_scalar(
                        rlv(i2), pb[2 + i2][:, :], dtb_s[:, db:db + 1], 0.0,
                        ALU.add, ALU.max), waits=[mm_ev, sp_war[i2]])
                    dt_ev = P.add("dve", lambda e, db=db, i2=i2: e.tensor_copy(
                        dtv(db), rlv(i2)), inc="Dv")
                    P.add("dve", lambda e, db=db: e.tensor_mul(wv_(db), dtv(db), xcm(db)),
                          inc="Dv")
                    dt_ready[db] = dt_ev
                    sp_war[i2] = dt_ev
                    psdt_war[i2] = dt_ev
                    continue
                if PART == "dtact":
                    P.add("act", lambda e, db=db, i2=i2: e.activation(
                        sp0v(i2), pb[2 + i2][:, :], AF.Abs, bias=dtb_s[:, db:db + 1]),
                        waits=[mm_ev, sp_war[i2]])
                    P.add("act", lambda e, i2=i2: e.activation(sp1v(i2), sp0v(i2), AF.Exp, scale=-1.0))
                    ab_ev = P.add("act", lambda e, i2=i2: e.activation(
                        sp0v(i2), sp1v(i2), AF.Ln, bias=1.0), inc="Ac")
                    dt_ev = P.add("dve", lambda e, db=db, i2=i2: e.tensor_copy(
                        dtv(db), sp0v(i2)), waits=[ab_ev], inc="Dv")
                    P.add("dve", lambda e, db=db: e.tensor_mul(wv_(db), dtv(db), xcm(db)),
                          inc="Dv")
                    dt_ready[db] = dt_ev
                    sp_war[i2] = dt_ev
                    psdt_war[i2] = dt_ev
                    continue
                ev_usb = P.add("dve", lambda e, db=db, i2=i2: e.tensor_scalar_add(
                    usbv(i2), pb[2 + i2][:, :], dtb_s[:, db:db + 1]),
                    waits=[mm_ev, sp_war[i2]], inc="Dv")
                P.add("dve", lambda e, i2=i2: e.tensor_scalar_max(
                    rlv(i2), usbv(i2), 0.0))
                P.add("act", lambda e, i2=i2: e.activation(sp0v(i2), usbv(i2), AF.Abs),
                      waits=[ev_usb, sp_war[i2]])
                P.add("act", lambda e, i2=i2: e.activation(sp1v(i2), sp0v(i2), AF.Exp, scale=-1.0))
                ab_ev = P.add("act", lambda e, i2=i2: e.activation(
                    sp0v(i2), sp1v(i2), AF.Ln, bias=1.0), inc="Ac")
                dt_ev = P.add("dve", lambda e, db=db, i2=i2: e.tensor_add(
                    dtv(db), rlv(i2), sp0v(i2)), waits=[ab_ev], inc="Dv")
                P.add("dve", lambda e, db=db: e.tensor_mul(wv_(db), dtv(db), xcm(db)), inc="Dv")
                dt_ready[db] = dt_ev
                sp_war[i2] = dt_ev
                psdt_war[i2] = ev_usb
            if PART in ("dtw", "prep") and tau < 2:
                P.add("sync", lambda e, tau=tau: e.dma_start(
                    out=out[0:1024, tau * TC:(tau + 1) * TC].rearrange("(db p) t -> p db t", p=128),
                    in_=a32[:, o_xcm:o_xcm + DBM * TC].rearrange("p (db t) -> p db t", db=DBM)),
                    waits=[dt_ready[DBM - 1], P.ev_last_dv()], inc=f"Ly{tau}")
                P.add("pool", lambda e, tau=tau: e.dma_start(
                    out=out[1024:2048, tau * TC:(tau + 1) * TC].rearrange("(db p) t -> p db t", p=128),
                    in_=a16[:, o_xcb:o_xcb + DBM * TC].rearrange("p (db t) -> p db t", db=DBM)),
                    waits=[dt_ready[DBM - 1]], inc="Gy")
            if PART in ("prep", "dtmm", "dtrl", "dtact", "xtd"):
                chunk_bar = [proj_ev, ev_dtr, ev_bc, dt_ready[DBM - 1], y_init_ev[DBM - 1]]
                continue
            scan_ev = None
            last_evx = None
            yadd_last = None
            for s in range(S):
                s2 = s % 2
                evB = P.add("pe", lambda e, s=s, s2=s2: e.matmul(
                    pb[4 + s2][:, :], esel_s[:, s * 128:(s + 1) * 128], bc_s[:],
                    start=True, stop=True),
                    waits=[ev_bc, psB_war[s2], ev_flush], inc="Pe")
                evC = P.add("pe", lambda e, s=s, s2=s2: e.matmul(
                    pb[6 + s2][:, :], esel2_s[:, s * 128:(s + 1) * 128], bc_s[:],
                    start=True, stop=True), waits=[ev_bc, ev_flush, psC_war[s2]], inc="Pe")
                prev_evC_last = evC
                evCe = P.add("act", lambda e, s2=s2: e.copy(csbv(s2), pb[6 + s2][:, :]),
                             waits=[evC, csb_war[s2]], inc="Ac")
                psC_war[s2] = evCe
                for db in range(DBM):
                    i = s * 8 + db
                    ii = i % 4
                    evdA = P.add("act", lambda e, db=db, ii=ii, col=db * 16 + s: e.activation(
                        dAv(ii), dtv(db), AF.Exp, scale=A_s[:, col:col + 1]),
                        waits=[dt_ready[db], dA_war[ii]], inc="Ac")
                    P.add("dve", lambda e, db=db, ii=ii, s2=s2: e.tensor_mul(
                        uv(ii), wv_(db), pb[4 + s2][:, :]), waits=[evB])
                    P.add("dve", lambda e, ii=ii: e.tensor_scalar_max(dbxv(ii), uv(ii), 0.0))
                    init = 0.0 if tau == 0 else state[:, i:i + 1]
                    evh = P.add("dve", lambda e, ii=ii, init=init: e.tensor_tensor_scan(
                        hv(ii), dAv(ii), dbxv(ii), init, ALU.mult, ALU.add),
                        waits=[evdA, h_war_pool[ii], h_war_act[ii]], inc="Dv")
                    scan_ev = evh
                    evx = P.add("act", lambda e, ii=ii, i=i: e.copy(
                        state[:, i:i + 1], hv(ii)[:, TC - 1:TC]), waits=[evh], inc="Ac")
                    last_evx = evx
                    P.add(SCAN_V_ENG, lambda e, ii=ii, s2=s2: e.tensor_mul(
                        vv(ii), hv(ii), csbv(s2)),
                        waits=[evh, evCe] + ([y_init_ev[db]] if s == 0 else []))
                    evy = P.add(SCAN_V_ENG, lambda e, ii=ii, db=db: e.tensor_add(
                        yv(db), yv(db), vv(ii)), inc="Po" if SCAN_V_ENG == "pool" else "Dv")
                    yadd_last = evy
                    dA_war[ii] = evh
                    h_war_pool[ii] = evy
                    h_war_act[ii] = evx
                    v_war[ii] = evy
                psB_war[s2] = scan_ev
                csb_war[s2] = yadd_last
            yb_ev = P.add("sync", lambda e, tau=tau: e.dma_start(
                out=ybin[tau].ap().rearrange("(db p) t -> p db t", p=128),
                in_=a32[:, o_y:o_y + DBM * TC].rearrange("p (db t) -> p db t", db=DBM)),
                waits=[yadd_last], inc=f"Ly{tau}")
            ag_ev = P.add("pool", lambda e, tau=tau: e.collective_compute(
                "AllGather", ALU.bypass, replica_groups=PAIRS,
                ins=[ybin[tau].ap().opt()],
                outs=[ybag.ap()[tau * D:(tau + 1) * D, :].opt()]),
                waits=[yb_ev], inc="Cc")
            ag_evs.append(ag_ev)
            prev_cb2 = None
            chunk_bar = [scan_ev, yadd_last, yb_ev, proj_ev, ev_dtr, ev_bc, last_evx]

        if PART == "dtw":
            prev = ag_evs[-1]
            P.add("sync", lambda e: e.nop(), waits=[prev])
        if PART == "mixer":
            prev = ag_evs[-1]
            for q in range(2 * HG):
                eq = P.add("pool", lambda e, q=q: e.dma_start(
                    out=f32v(o_yf + (q % HG) * TC, TC),
                    in_=ybag.ap()[q * 128:(q + 1) * 128, :]),
                    waits=[prev], inc="Gy")
                prev = P.add("sync", lambda e, q=q: e.dma_start(
                    out=out[(q % HG) * 128:((q % HG) + 1) * 128, (q // HG) * TC:(q // HG + 1) * TC],
                    in_=f32v(o_yf + (q % HG) * TC, TC)), waits=[eq], inc="Lo0")

        all_ag = ag_evs[-1] if ag_evs else None
        slab_bar = []
        for sl in (range(2) if PART == "all" else ()):
            def yf_load(e, k=sl * 16):
                v = e.value_load(offs_s[0:1, k:k + 1])
                src = ybag.ap()[bass.ds(v, D), :].rearrange("(hg p) t -> p hg t", p=128)
                return e.dma_start(
                    out=a32[:, o_yf:o_yf + HG * TC].rearrange("p (hg t) -> p hg t", hg=HG),
                    in_=src)
            ev_yf = P.add("pool", yf_load,
                          waits=[all_ag, ev_flush] + slab_bar, inc="Gy")
            yf_evs = [ev_yf] * HG

            def emit_ln_stats(src_v, pre_evs, extra0):
                sw = [None, None]
                ev2 = None
                for hg in range(HG):
                    evq = P.add("act", lambda e, hg=hg: e.activation(
                        sqv(hg % 2), src_v(hg), AF.Square),
                        waits=[pre_evs[hg], sw[hg % 2]] + (extra0 if hg < 2 else []), inc="Ac")
                    P.add("pe", lambda e, hg=hg: e.matmul(
                        pb[0][0:1, :], one128_s[:], src_v(hg),
                        start=(hg == 0), stop=(hg == HG - 1)),
                        waits=[pre_evs[hg], ev_flush] + (extra0 if hg == 0 else []))
                    ev2 = P.add("pe", lambda e, hg=hg: e.matmul(
                        pb[1][0:1, :], one128_s[:], sqv(hg % 2),
                        start=(hg == 0), stop=(hg == HG - 1)),
                        waits=[evq], inc="Pe")
                    sw[hg % 2] = ev2
                return ev2, ev2

            def emit_ln_rows(stat_ev):
                P.add("dve", lambda e: e.tensor_scalar_mul(row(0), pb[0][0:1, :], INV_D),
                      waits=[stat_ev])
                P.add("dve", lambda e: e.tensor_scalar_mul(row(3), pb[1][0:1, :], INV_D))
                P.add("dve", lambda e: e.tensor_mul(row(2), row(0), row(0)))
                P.add("dve", lambda e: e.tensor_sub(row(3), row(3), row(2)))
                ev_var = P.add("dve", lambda e: e.tensor_scalar_add(row(3), row(3), EPS_LN),
                               inc="Dv")
                ev_ln = P.add("act", lambda e: e.activation(row(2), row(3), AF.Ln),
                              waits=[ev_var], inc="Ac")
                ev_hf = P.add("dve", lambda e: e.tensor_scalar_mul(row(2), row(2), -0.5),
                              waits=[ev_ln], inc="Dv")
                ev_rstd = P.add("act", lambda e: e.activation(row(1), row(2), AF.Exp),
                                waits=[ev_hf], inc="Ac")
                ev_b = P.add("dve", lambda e: e.scalar_tensor_tensor(
                    row(2), row(0), -1.0, row(1), ALU.mult, ALU.mult),
                    waits=[ev_rstd], inc="Dv")
                eva_r = P.add("pe", lambda e: e.matmul(pb[2][:, :], onec_s[:], row(1),
                                                       start=True, stop=True),
                              waits=[ev_b, ev_flush], inc="Pe")
                evb_r = P.add("pe", lambda e: e.matmul(pb[3][:, :], onec_s[:], row(2),
                                                       start=True, stop=True), inc="Pe")
                return ev_b, eva_r, evb_r

            st1, st2 = emit_ln_stats(yfv, yf_evs, slab_bar)
            ev_b, eva_r, evb_r = emit_ln_rows(st2)
            x1_evs = []
            for hg in range(HG):
                P.add("dve", lambda e, hg=hg: e.tensor_mul(yfv(hg), yfv(hg), pb[2][:, :]),
                      waits=[evb_r])
                P.add("dve", lambda e, hg=hg: e.tensor_add(yfv(hg), yfv(hg), pb[3][:, :]))
                x1_evs.append(P.add("dve", lambda e, hg=hg: e.tensor_copy(x1bv(hg), yfv(hg)),
                                    inc="Dv"))
            wgu_war = [None, None]
            gu_done = [None, None]
            ft_war = [None, None]
            gt_war = [None, None]
            fdma_evs = []
            for it in range(NIT):
                ii = it % 2
                evw = P.add("pool", lambda e, it=it, ii=ii: e.dma_start(
                    out=b16v(o_wgu[ii], 4096), in_=hgu[it]), waits=[wgu_war[ii]], inc=f"Gw{ii}")
                ba, bb = 4 + 2 * ii, 5 + 2 * ii
                evg = evu = None
                for kg in range(DBA):
                    st, sp = kg == 0, kg == DBA - 1
                    evg = P.add("pe", lambda e, kg=kg, st=st, sp=sp, ba=ba, ii=ii: e.matmul(
                        pb[ba][:, :], wguv(ii, kg * 128, 128), x1bv(kg), start=st, stop=sp),
                        waits=[evw, x1_evs[kg]] + ([gu_done[ii]] if st else []),
                        inc="Pe" if sp else None) or evg
                    evu = P.add("pe", lambda e, kg=kg, st=st, sp=sp, bb=bb, ii=ii: e.matmul(
                        pb[bb][:, :], wguv(ii, 2048 + kg * 128, 128), x1bv(kg),
                        start=st, stop=sp), inc="Pe" if sp else None) or evu
                wgu_war[ii] = evu
                evgt = P.add("act", lambda e, it=it, ii=ii, ba=ba: e.activation(
                    gtv(ii), pb[ba][:, :], AF.Silu, scale=sg_s[:, it:it + 1]),
                    waits=[evg, evu, gt_war[ii]], inc="Ac")
                evft = P.add("dve", lambda e, it=it, ii=ii, bb=bb: e.scalar_tensor_tensor(
                    ftv(ii), pb[bb][:, :], su_s[:, it:it + 1], gtv(ii), ALU.mult, ALU.mult),
                    waits=[evgt, ft_war[ii]], inc="Dv")
                gu_done[ii] = evft
                gt_war[ii] = evft
                evfd = P.add("sync", lambda e, it=it, ii=ii: e.dma_start(
                    out=fuse_d[it], in_=ftv(ii)), waits=[evft], inc=f"Lf{ii}")
                fdma_evs.append(evfd)
                ft_war[ii] = evfd
            ev_evac_all = [None] * HG
            prev_evac = []
            wd_war = [None, None]
            frd_war = [None, None]
            for hh in range(2):
                down_done = [None] * 8
                evd = None
                for it in range(NIT):
                    ii = it % 2
                    evwd = P.add("pool", lambda e, it=it, ii=ii, hh=hh: e.dma_start(
                        out=b16v(o_wd[ii], 1024), in_=hd[it, :, hh * 1024:(hh + 1) * 1024]),
                        waits=[wd_war[ii]], inc=f"Gv{ii}")
                    evfr = P.add("sync", lambda e, it=it, ii=ii: e.dma_start(
                        out=frdv(ii), in_=fuse_d[it]),
                        waits=[fdma_evs[-2], fdma_evs[-1], frd_war[ii]], inc=f"Lr{ii}")
                    for g8 in range(8):
                        st, sp = it == 0, it == NIT - 1
                        w0 = [evwd, evfr]
                        if st:
                            w0 = w0 + [ev_b, x1_evs[-1]] + prev_evac
                            if hh == 0:
                                w0 = w0 + [gu_done[0], gu_done[1]]
                        evd = P.add("pe", lambda e, ii=ii, g8=g8, st=st, sp=sp: e.matmul(
                            pb[g8][:, :], wdv(ii, g8 * 128, 128), frdv(ii),
                            start=st, stop=sp), waits=w0,
                            inc="Pe" if (g8 == 7 or sp) else None) or evd
                        if sp:
                            down_done[g8] = evd
                    wd_war[ii] = evd
                    frd_war[ii] = evd
                cur = []
                for g8 in range(8):
                    hg = hh * 8 + g8
                    eve = P.add("dve", lambda e, g8=g8, hg=hg: e.scalar_tensor_tensor(
                        y2v(hg), pb[g8][:, :], sd_s[:, hg:hg + 1], yfv(hg),
                        ALU.mult, ALU.add), waits=[down_done[g8]], inc="Dv")
                    cur.append(eve)
                    ev_evac_all[hg] = eve
                prev_evac = cur
            st1, st2 = emit_ln_stats(y2v, ev_evac_all, ev_evac_all)
            ev_b2, eva_r2, evb_r2 = emit_ln_rows(st2)
            ost_war = [None, None]
            slab_bar = []
            for hg in range(HG):
                P.add("dve", lambda e, hg=hg: e.tensor_mul(ostv(hg % 2), y2v(hg), pb[2][:, :]),
                      waits=[evb_r2, ost_war[hg % 2]])
                evo = P.add("dve", lambda e, hg=hg: e.tensor_add(
                    ostv(hg % 2), ostv(hg % 2), pb[3][:, :]), inc="Dv")
                evod = P.add("sync", lambda e, hg=hg, sl=sl: e.dma_start(
                    out=out[hg * 128:(hg + 1) * 128, sl * TC:(sl + 1) * TC], in_=ostv(hg % 2)),
                    waits=[evo], inc=f"Lo{hg % 2}")
                ost_war[hg % 2] = evod
                slab_bar.append(evod)
            slab_bar.append(ev_evac_all[-1])
            slab_bar.append(st2)

        P.add("sync", lambda e: e.nop(),
              waits=[(s, P.cnt[s]) for s in list(P.cnt) if is_dma_sem(s)])

        with nc.Block() as block:
            @block.sync
            def _(eng):
                P.emit_engine("sync", eng, sems)

            @block.tensor
            def _(eng):
                P.emit_engine("pe", eng, sems)

            @block.scalar
            def _(eng):
                P.emit_engine("act", eng, sems)

            @block.vector
            def _(eng):
                P.emit_engine("dve", eng, sems)

            @block.gpsimd
            def _(eng):
                P.emit_engine("pool", eng, sems)

    return nc


_NC_CACHE = {}


def _get_nc():
    if "nc" not in _NC_CACHE:
        _NC_CACHE["nc"] = build_nc()
    return _NC_CACHE["nc"]


def kernel(L0_attn_x, L0_attn_conv_w, L0_attn_conv_b, L0_attn_x_proj_w,
           L0_attn_dt_proj_w, L0_attn_dt_proj_b, L0_attn_A_log, L0_attn_D_param,
           L0_ffn_w_gate, L0_ffn_w_up, L0_ffn_w_down, L0_ffn_s_gate,
           L0_ffn_s_up, L0_ffn_s_down):
    x = np.asarray(L0_attn_x, np.float32)
    conv_w = np.asarray(L0_attn_conv_w, np.float32)
    conv_b = np.asarray(L0_attn_conv_b, np.float32)
    x_proj_w = np.asarray(L0_attn_x_proj_w, np.float32)
    dt_proj_w = np.asarray(L0_attn_dt_proj_w, np.float32)
    dt_proj_b = np.asarray(L0_attn_dt_proj_b, np.float32)
    A_log = np.asarray(L0_attn_A_log, np.float32)
    D_param = np.asarray(L0_attn_D_param, np.float32)
    wg8 = np.asarray(L0_ffn_w_gate)
    wu8 = np.asarray(L0_ffn_w_up)
    wd8 = np.asarray(L0_ffn_w_down)
    s_gate = np.asarray(L0_ffn_s_gate, np.float32)
    s_up = np.asarray(L0_ffn_s_up, np.float32)
    s_down = np.asarray(L0_ffn_s_down, np.float32)

    def half_prep(role):
        lo = role * 1024
        perm = np.concatenate([np.arange(lo, lo + 1024),
                               np.arange((1 - role) * 1024, (1 - role) * 1024 + 1024)])
        d = {}
        d["cw"] = np.ascontiguousarray(
            conv_w[perm, 0, :].reshape(16, 128, 4).transpose(1, 0, 2).reshape(128, 64))
        d["cbi"] = np.ascontiguousarray(conv_b[perm].reshape(16, 128).T)
        d["dpi"] = np.ascontiguousarray(D_param[lo:lo + 1024].reshape(8, 128).T)
        d["dtbi"] = np.ascontiguousarray(dt_proj_b[lo:lo + 1024].reshape(8, 128).T)
        d["alog"] = np.ascontiguousarray(
            A_log[lo:lo + 1024].reshape(8, 128, 16).transpose(1, 0, 2).reshape(128, 128))
        d["wxp"] = np.ascontiguousarray(x_proj_w.T[perm])
        d["wdt"] = np.ascontiguousarray(dt_proj_w[lo:lo + 1024].T)
        off = np.zeros((1, 32), np.uint32)
        for sl in range(2):
            for hg in range(HG):
                off[0, sl * 16 + hg] = (role * 2 + sl) * D + hg * 128
        d["offs"] = off
        return d, perm

    halves = [half_prep(0), half_prep(1)]

    esel = np.zeros((32, 2048), np.float32)
    esel2 = np.zeros((32, 2048), np.float32)
    for s in range(16):
        esel[s, s * 128:(s + 1) * 128] = 1.0
        esel2[16 + s, s * 128:(s + 1) * 128] = 1.0
    onescol = np.ones((1, 128), np.float32)
    ones128 = np.ones((128, 1), np.float32)
    g4 = wg8.reshape(16, 128, 64, 128).transpose(2, 1, 0, 3)
    u4 = wu8.reshape(16, 128, 64, 128).transpose(2, 1, 0, 3)
    hgu = np.ascontiguousarray(np.stack([g4, u4], axis=2).reshape(NIT, 128, 4096))
    hd = np.ascontiguousarray(wd8.reshape(NIT, 128, 2048))
    sgl = np.ascontiguousarray(s_gate.reshape(64, 128).T)
    sul = np.ascontiguousarray(s_up.reshape(64, 128).T)
    sdl = np.ascontiguousarray(s_down.reshape(16, 128).T)

    in_maps = []
    for c in range(8):
        b, role = c // 2, c % 2
        d, perm = halves[role]
        xTh = np.zeros((D, T + 3), np.float32)
        xTh[:, 3:] = x[b].T[perm]
        m = {"xT": xTh, "esel": esel, "esel2": esel2, "onescol": onescol, "ones128": ones128,
             "hgu": hgu, "hd": hd, "sgd": sgl, "sud": sul, "sdd": sdl}
        m.update(d)
        in_maps.append(m)

    nc = _get_nc()
    res = run_bass_kernel_spmd(nc, in_maps, core_ids=list(range(8)))

    result = np.empty((BATCH, T, D), np.float32)
    for c in range(8):
        b, role = c // 2, c % 2
        o = np.asarray(res.results[c]["out"])
        result[b, role * 1024:(role + 1) * 1024, :] = o.T
    return result




# version 05
# speedup vs baseline: 1.0732x; 1.0672x over previous
import contextlib
import numpy as np
import ml_dtypes
import concourse.bass as bass
import concourse.mybir as mybir
from concourse.bass_utils import run_bass_kernel_spmd

F32 = mybir.dt.float32
BF16 = mybir.dt.bfloat16
I8 = mybir.dt.int8
U32 = mybir.dt.uint32
AF = mybir.ActivationFunctionType
ALU = mybir.AluOpType

D = 2048
T = 2048
BATCH = 4
TC = 512
NCH = T // TC
DBA = 16
DBM = 8
S = 16
RK = 128
NIT = 64
HG = 16
INV_D = 1.0 / D
EPS_LN = 1e-5
PAIRS = [[0, 1], [2, 3], [4, 5], [6, 7]]

ENGINES = ("sync", "pe", "act", "dve", "pool")
OWN_SEM = {"pe": "Pe", "act": "Ac", "dve": "Dv", "pool": "Po"}
def is_dma_sem(s):
    return s[0] in ("L", "G")


class Prog:

    def __init__(self):
        self.ops = []
        self.cnt = {}

    def ev_last_dv(self):
        return ("Dv", self.cnt.get("Dv", 0))

    def add(self, eng, fn, waits=(), inc=None):
        ev = None
        if inc is not None:
            self.cnt[inc] = self.cnt.get(inc, 0) + 1
            ev = (inc, self.cnt[inc])
        self.ops.append((eng, fn, tuple(w for w in waits if w is not None), inc))
        return ev

    def emit_engine(self, name, eng, sems):
        known = {}
        own = OWN_SEM.get(name)
        for oeng, fn, waits, inc in self.ops:
            if oeng != name:
                continue
            for sem, val in waits:
                if sem.endswith("!"):
                    sem = sem[:-1]
                elif sem == own:
                    continue
                tgt = val * 16 if is_dma_sem(sem) else val
                if known.get(sem, 0) < tgt:
                    eng.wait_ge(sems[sem], tgt)
                    known[sem] = tgt
            ins = fn(eng)
            if inc is not None:
                ins.then_inc(sems[inc], 16 if is_dma_sem(inc) else 1)


PART = "all"
SCAN_V_ENG = "pool"


def build_nc():
    nc = bass.Bass("TRN2", target_bir_lowering=False, debug=False)

    xT = nc.dram_tensor("xT", [D, T + 3], F32, kind="ExternalInput").ap()
    cw = nc.dram_tensor("cw", [128, 64], F32, kind="ExternalInput").ap()
    cbi = nc.dram_tensor("cbi", [128, 16], F32, kind="ExternalInput").ap()
    dpi = nc.dram_tensor("dpi", [128, 8], F32, kind="ExternalInput").ap()
    dtbi = nc.dram_tensor("dtbi", [128, 8], F32, kind="ExternalInput").ap()
    alog = nc.dram_tensor("alog", [128, 128], F32, kind="ExternalInput").ap()
    wxp = nc.dram_tensor("wxp", [D, 160], F32, kind="ExternalInput").ap()
    wdt = nc.dram_tensor("wdt", [RK, 1024], F32, kind="ExternalInput").ap()
    esel = nc.dram_tensor("esel", [32, 2048], F32, kind="ExternalInput").ap()
    esel2 = nc.dram_tensor("esel2", [32, 2048], F32, kind="ExternalInput").ap()
    onescol = nc.dram_tensor("onescol", [1, 128], F32, kind="ExternalInput").ap()
    ones128 = nc.dram_tensor("ones128", [128, 1], F32, kind="ExternalInput").ap()
    hgu = nc.dram_tensor("hgu", [NIT, 128, 4096], I8, kind="ExternalInput").ap()
    hd = nc.dram_tensor("hd", [NIT, 128, 2048], I8, kind="ExternalInput").ap()
    sgd = nc.dram_tensor("sgd", [128, 64], F32, kind="ExternalInput").ap()
    sud = nc.dram_tensor("sud", [128, 64], F32, kind="ExternalInput").ap()
    sdd = nc.dram_tensor("sdd", [128, 16], F32, kind="ExternalInput").ap()
    offs = nc.dram_tensor("offs", [1, 32], U32, kind="ExternalInput").ap()

    out = nc.dram_tensor("out", [D, 1024], F32, kind="ExternalOutput").ap()

    ybin = [nc.dram_tensor(f"ybin{t}", [1024, TC], F32) for t in range(NCH)]
    ybag = nc.dram_tensor("ybag", [NCH * D, TC], F32)
    fuse_d = nc.dram_tensor("fuse_d", [NIT, 128, TC], BF16)

    ctx = contextlib.ExitStack()
    sb = lambda name, shape, dt: ctx.enter_context(nc.sbuf_tensor(name, shape, dt))
    psb = lambda name, shape: ctx.enter_context(nc.psum_tensor(name, shape, F32))

    with ctx:
        cw_s = sb("cw_s", [128, 64], F32)
        cb_s = sb("cb_s", [128, 16], F32)
        dp_s = sb("dp_s", [128, 8], F32)
        dtb_s = sb("dtb_s", [128, 8], F32)
        A_s = sb("A_s", [128, 128], F32)
        Atmp_s = sb("Atmp_s", [128, 128], F32)
        flsh_s = sb("flsh_s", [128, 128], F32)
        wxp_s = sb("wxp_s", [128, 16 * 160], BF16)
        wdt_s = sb("wdt_s", [128, 1024], BF16)
        esel_s = sb("esel_s", [32, 2048], F32)
        esel2_s = sb("esel2_s", [32, 2048], F32)
        onec_s = sb("onec_s", [1, 128], F32)
        one128_s = sb("one128_s", [128, 1], F32)
        sg_s = sb("sg_s", [128, 64], F32)
        su_s = sb("su_s", [128, 64], F32)
        sd_s = sb("sd_s", [128, 16], F32)
        offs_s = sb("offs_s", [1, 32], U32)
        state = sb("state", [128, 128], F32)
        bc_s = sb("bc_s", [32, TC], F32)
        dtr_s = sb("dtr_s", [128, TC], BF16)

        AB = 158 * 1024
        arena = sb("arena", [128, AB], I8)
        a32 = arena.bitcast(F32)
        a16 = arena.bitcast(BF16)

        class Alloc:
            def __init__(self):
                self.cur = 0

            def f32(self, nelem):
                off = (self.cur + 3) // 4 * 4
                self.cur = off + nelem * 4
                assert self.cur <= AB, f"arena overflow {self.cur}"
                return off // 4

            def bf16(self, nelem):
                off = (self.cur + 1) // 2 * 2
                self.cur = off + nelem * 2
                assert self.cur <= AB, f"arena overflow {self.cur}"
                return off // 2

        M = Alloc()
        o_xt = [M.f32(TC + 3) for _ in range(4)]
        o_acc = [M.f32(TC) for _ in range(2)]
        o_acc2 = [M.f32(TC) for _ in range(2)]
        o_xcm = M.f32(DBM * TC)
        o_xcb = M.bf16(DBA * TC)
        o_dt = M.f32(DBM * TC)
        o_w = M.f32(DBM * TC)
        o_y = M.f32(DBM * TC)
        o_dA = [M.f32(TC) for _ in range(4)]
        o_dbx = [M.f32(TC) for _ in range(4)]
        o_u = [M.f32(TC) for _ in range(4)]
        o_h = [M.f32(TC) for _ in range(4)]
        o_v = [M.f32(TC) for _ in range(4)]
        o_csb = [M.f32(TC) for _ in range(2)]
        o_sp0 = [M.f32(TC) for _ in range(2)]
        o_sp1 = [M.f32(TC) for _ in range(2)]
        o_rl = [M.f32(TC) for _ in range(2)]
        o_usb = [M.f32(TC) for _ in range(2)]

        F = Alloc()
        o_yf = F.f32(HG * TC)
        o_y2 = F.f32(HG * TC)
        o_x1b = F.bf16(HG * TC)
        o_wgu = [F.bf16(4096) for _ in range(2)]
        o_wd = [F.bf16(1024) for _ in range(4)]
        o_frd = [F.bf16(TC) for _ in range(4)]
        o_gt = [F.f32(TC) for _ in range(2)]
        o_ft = [F.bf16(TC) for _ in range(2)]
        o_sq = [F.f32(TC) for _ in range(2)]
        o_rows = F.f32(4 * TC)
        o_ost = [F.f32(TC) for _ in range(2)]

        f32v = lambda off, n, p0=0, p1=128: a32[p0:p1, off:off + n]
        b16v = lambda off, n, p0=0, p1=128: a16[p0:p1, off:off + n]

        xtv = lambda i: f32v(o_xt[i], TC + 3)
        xts = lambda i, a, n: f32v(o_xt[i] + a, n)
        accv = lambda i: f32v(o_acc[i], TC)
        acc2v = lambda i: f32v(o_acc2[i], TC)
        xcm = lambda db: f32v(o_xcm + db * TC, TC)
        xcb = lambda db: b16v(o_xcb + db * TC, TC)
        dtv = lambda db: f32v(o_dt + db * TC, TC)
        wv_ = lambda db: f32v(o_w + db * TC, TC)
        yv = lambda db: f32v(o_y + db * TC, TC)
        dAv = lambda i: f32v(o_dA[i], TC)
        dbxv = lambda i: f32v(o_dbx[i], TC)
        uv = lambda i: f32v(o_u[i], TC)
        hv = lambda i: f32v(o_h[i], TC)
        vv = lambda i: f32v(o_v[i], TC)
        csbv = lambda i: f32v(o_csb[i], TC)
        sp0v = lambda i: f32v(o_sp0[i], TC)
        sp1v = lambda i: f32v(o_sp1[i], TC)
        rlv = lambda i: f32v(o_rl[i], TC)
        usbv = lambda i: f32v(o_usb[i], TC)

        yfv = lambda hg: f32v(o_yf + hg * TC, TC)
        y2v = lambda hg: f32v(o_y2 + hg * TC, TC)
        x1bv = lambda hg: b16v(o_x1b + hg * TC, TC)
        wguv = lambda i, a, n: b16v(o_wgu[i] + a, n)
        wdv = lambda i, a, n: b16v(o_wd[i] + a, n)
        frdv = lambda i: b16v(o_frd[i], TC)
        gtv = lambda i: f32v(o_gt[i], TC)
        ftv = lambda i: b16v(o_ft[i], TC)
        sqv = lambda i: f32v(o_sq[i], TC)
        row = lambda i: a32[0:1, o_rows + i * TC: o_rows + (i + 1) * TC]
        ostv = lambda i: f32v(o_ost[i], TC)

        pb = [psb(f"pb{i}", [128, 512]) for i in range(8)]

        sem_names = (["Lc", "Gc", "Gy", "Lf0", "Lf1", "Cc", "Pe", "Ac", "Dv", "Po"]
                     + [f"Lx{i}" for i in range(4)] + [f"Ly{i}" for i in range(NCH)]
                     + [f"Lr{i}" for i in range(4)] + [f"Lo{i}" for i in range(2)]
                     + [f"Gw{i}" for i in range(2)] + [f"Gv{i}" for i in range(4)])
        sems = {n: ctx.enter_context(nc.semaphore(n)) for n in sem_names}

        P = Prog()

        const_loads = [("cw", cw_s, cw), ("cb", cb_s, cbi), ("dp", dp_s, dpi),
                       ("dtb", dtb_s, dtbi), ("alog", Atmp_s, alog),
                       ("esel", esel_s, esel), ("esel2", esel2_s, esel2), ("onec", onec_s, onescol),
                       ("one128", one128_s, ones128), ("sg", sg_s, sgd),
                       ("su", su_s, sud), ("sd", sd_s, sdd), ("offs", offs_s, offs)]
        const_evs = {}
        for name, dst, src in const_loads:
            const_evs[name] = P.add("sync", lambda e, d=dst, s=src: e.dma_start(out=d[:], in_=s), inc="Lc")
        ev_wxp = P.add("pool", lambda e: e.dma_start(
            out=wxp_s.rearrange("p (kg f) -> p kg f", kg=16),
            in_=wxp.rearrange("(kg p) f -> p kg f", p=128)), inc="Gc")
        ev_wdt = P.add("pool", lambda e: e.dma_start(out=wdt_s[:], in_=wdt), inc="Gc")
        ev_flush = P.add("sync", lambda e: e.dma_start(out=flsh_s[:], in_=alog), inc="Lc")
        evA1 = P.add("act", lambda e: e.activation(A_s[:], Atmp_s[:], AF.Exp),
                     waits=[ev_flush, const_evs["alog"]], inc="Ac")
        evA = P.add("dve", lambda e: e.tensor_scalar_mul(A_s[:], A_s[:], -1.0),
                    waits=[evA1], inc="Dv")

        xt_war = [None] * 4
        acc_war = [None] * 2
        dA_war = [None] * 4
        h_war_pool = [None] * 4
        h_war_act = [None] * 4
        v_war = [None] * 4
        dbx_war = [None] * 4
        csb_war = [None] * 2
        sp_war = [None] * 2
        psB_war = [None] * 2
        psC_war = [None] * 2
        psdt_war = [None] * 2
        ag_evs = []
        dt_ready = [None] * DBM
        y_init_ev = [None] * DBM
        chunk_bar = []
        prev_cb2 = None
        prev_evC_last = None

        for tau in range(NCH):
            t0 = tau * TC
            xc_ready = [None] * DBA
            for db in range(DBA):
                it4 = db % 4
                ld_ev = P.add("sync", lambda e, i=it4, r0=db * 128, c0=t0: e.dma_start(
                    out=xtv(i), in_=xT[r0:r0 + 128, c0:c0 + TC + 3]),
                    waits=[xt_war[it4]] + (chunk_bar if db < 4 else []), inc=f"Lx{it4}")
                i2 = db % 2
                P.add("dve", lambda e, i2=i2, i=it4, db=db: e.tensor_scalar(
                    accv(i2), xts(i, 0, TC), cw_s[:, db * 4:db * 4 + 1], None, ALU.mult),
                    waits=[ld_ev, acc_war[i2], evA] + (chunk_bar if db < 2 else []))
                P.add("dve", lambda e, i2=i2, i=it4, db=db: e.scalar_tensor_tensor(
                    acc2v(i2), xts(i, 1, TC), cw_s[:, db * 4 + 1:db * 4 + 2],
                    accv(i2), ALU.mult, ALU.add))
                P.add("dve", lambda e, i2=i2, i=it4, db=db: e.scalar_tensor_tensor(
                    accv(i2), xts(i, 2, TC), cw_s[:, db * 4 + 2:db * 4 + 3],
                    acc2v(i2), ALU.mult, ALU.add))
                tap_ev = P.add("dve", lambda e, i2=i2, i=it4, db=db: e.scalar_tensor_tensor(
                    acc2v(i2), xts(i, 3, TC), cw_s[:, db * 4 + 3:db * 4 + 4],
                    accv(i2), ALU.mult, ALU.add), inc="Dv")
                if PART == "xtd" and tau == 1 and db < 4:
                    xd_ev = P.add("sync", lambda e, i=it4, db=db: e.dma_start(
                        out=out[db * 128:(db + 1) * 128, 0:TC + 3], in_=xtv(i)),
                        waits=[tap_ev], inc=f"Ly{tau % NCH}")
                    tap_ev = xd_ev
                if db < DBM:
                    sl_ev = P.add("act", lambda e, i2=i2, db=db: e.activation(
                        xcm(db), acc2v(i2), AF.Silu, bias=cb_s[:, db:db + 1]),
                        waits=[tap_ev] + (chunk_bar if db < 2 else []), inc="Ac")
                    cp_ev = P.add("dve", lambda e, db=db: e.tensor_copy(xcb(db), xcm(db)),
                                  waits=[sl_ev], inc="Dv")
                    yi_ev = P.add("dve", lambda e, db=db, i=it4: e.scalar_tensor_tensor(
                        yv(db), xcm(db), dp_s[:, db:db + 1], xts(i, 3, TC),
                        ALU.mult, ALU.add), inc="Dv")
                    y_init_ev[db] = yi_ev
                    xc_ready[db] = cp_ev
                    xt_war[it4] = yi_ev
                    acc_war[i2] = cp_ev
                else:
                    sl_ev = P.add("act", lambda e, i2=i2, db=db: e.activation(
                        xcb(db), acc2v(i2), AF.Silu, bias=cb_s[:, db:db + 1]),
                        waits=[tap_ev], inc="Ac")
                    xc_ready[db] = sl_ev
                    xt_war[it4] = tap_ev
                    acc_war[i2] = sl_ev
            if PART == "conv":
                chunk_bar = [xc_ready[DBA - 1], y_init_ev[DBM - 1]]
                continue
            proj_ev = None
            for kg in range(DBA):
                st, sp = kg == 0, kg == DBA - 1
                P.add("pe", lambda e, kg=kg, st=st, sp=sp: e.matmul(
                    pb[0][:, :], wxp_s[:, kg * 160:kg * 160 + 128], xcb(kg), start=st, stop=sp),
                    waits=[xc_ready[kg], ev_wdt] + (chunk_bar if st else []),
                    inc="Pe" if sp else None)
                proj_ev = P.add("pe", lambda e, kg=kg, st=st, sp=sp: e.matmul(
                    pb[1][0:32, :], wxp_s[:, kg * 160 + 128:kg * 160 + 160], xcb(kg),
                    start=st, stop=sp), inc="Pe" if sp else None) or proj_ev
            ev_dtr = P.add("act", lambda e: e.copy(dtr_s[:], pb[0][:, :]),
                           waits=[proj_ev], inc="Ac")
            ev_bc = P.add("act", lambda e: e.copy(bc_s[:], pb[1][0:32, :]),
                          waits=[proj_ev, prev_cb2, prev_evC_last], inc="Ac")
            if PART == "proj":
                chunk_bar = [proj_ev, ev_dtr, ev_bc, y_init_ev[DBM - 1]]
                continue
            for db in range(DBM):
                i2 = db % 2
                mm_ev = P.add("pe", lambda e, db=db, i2=i2: e.matmul(
                    pb[2 + i2][:, :], wdt_s[:, db * 128:(db + 1) * 128], dtr_s[:],
                    start=True, stop=True),
                    waits=[ev_dtr, ev_wdt, psdt_war[i2]], inc="Pe")
                if PART == "dtmm":
                    dt_ev = P.add("act", lambda e, db=db, i2=i2: e.copy(
                        dtv(db), pb[2 + i2][:, :]), waits=[mm_ev, sp_war[i2]], inc="Ac")
                    P.add("dve", lambda e, db=db: e.tensor_mul(wv_(db), dtv(db), xcm(db)),
                          waits=[dt_ev], inc="Dv")
                    dt_ready[db] = dt_ev
                    sp_war[i2] = dt_ev
                    psdt_war[i2] = dt_ev
                    continue
                if PART == "dtrl":
                    rl_ev = P.add("dve", lambda e, db=db, i2=i2: e.tensor_scalar(
                        rlv(i2), pb[2 + i2][:, :], dtb_s[:, db:db + 1], 0.0,
                        ALU.add, ALU.max), waits=[mm_ev, sp_war[i2]])
                    dt_ev = P.add("dve", lambda e, db=db, i2=i2: e.tensor_copy(
                        dtv(db), rlv(i2)), inc="Dv")
                    P.add("dve", lambda e, db=db: e.tensor_mul(wv_(db), dtv(db), xcm(db)),
                          inc="Dv")
                    dt_ready[db] = dt_ev
                    sp_war[i2] = dt_ev
                    psdt_war[i2] = dt_ev
                    continue
                if PART == "dtact":
                    P.add("act", lambda e, db=db, i2=i2: e.activation(
                        sp0v(i2), pb[2 + i2][:, :], AF.Abs, bias=dtb_s[:, db:db + 1]),
                        waits=[mm_ev, sp_war[i2]])
                    P.add("act", lambda e, i2=i2: e.activation(sp1v(i2), sp0v(i2), AF.Exp, scale=-1.0))
                    ab_ev = P.add("act", lambda e, i2=i2: e.activation(
                        sp0v(i2), sp1v(i2), AF.Ln, bias=1.0), inc="Ac")
                    dt_ev = P.add("dve", lambda e, db=db, i2=i2: e.tensor_copy(
                        dtv(db), sp0v(i2)), waits=[ab_ev], inc="Dv")
                    P.add("dve", lambda e, db=db: e.tensor_mul(wv_(db), dtv(db), xcm(db)),
                          inc="Dv")
                    dt_ready[db] = dt_ev
                    sp_war[i2] = dt_ev
                    psdt_war[i2] = dt_ev
                    continue
                ev_usb = P.add("dve", lambda e, db=db, i2=i2: e.tensor_scalar_add(
                    usbv(i2), pb[2 + i2][:, :], dtb_s[:, db:db + 1]),
                    waits=[mm_ev, sp_war[i2]], inc="Dv")
                P.add("dve", lambda e, i2=i2: e.tensor_scalar_max(
                    rlv(i2), usbv(i2), 0.0))
                P.add("act", lambda e, i2=i2: e.activation(sp0v(i2), usbv(i2), AF.Abs),
                      waits=[ev_usb, sp_war[i2]])
                P.add("act", lambda e, i2=i2: e.activation(sp1v(i2), sp0v(i2), AF.Exp, scale=-1.0))
                ab_ev = P.add("act", lambda e, i2=i2: e.activation(
                    sp0v(i2), sp1v(i2), AF.Ln, bias=1.0), inc="Ac")
                dt_ev = P.add("dve", lambda e, db=db, i2=i2: e.tensor_add(
                    dtv(db), rlv(i2), sp0v(i2)), waits=[ab_ev], inc="Dv")
                P.add("dve", lambda e, db=db: e.tensor_mul(wv_(db), dtv(db), xcm(db)), inc="Dv")
                dt_ready[db] = dt_ev
                sp_war[i2] = dt_ev
                psdt_war[i2] = ev_usb
            if PART in ("dtw", "prep") and tau < 2:
                P.add("sync", lambda e, tau=tau: e.dma_start(
                    out=out[0:1024, tau * TC:(tau + 1) * TC].rearrange("(db p) t -> p db t", p=128),
                    in_=a32[:, o_xcm:o_xcm + DBM * TC].rearrange("p (db t) -> p db t", db=DBM)),
                    waits=[dt_ready[DBM - 1], P.ev_last_dv()], inc=f"Ly{tau}")
                P.add("pool", lambda e, tau=tau: e.dma_start(
                    out=out[1024:2048, tau * TC:(tau + 1) * TC].rearrange("(db p) t -> p db t", p=128),
                    in_=a16[:, o_xcb:o_xcb + DBM * TC].rearrange("p (db t) -> p db t", db=DBM)),
                    waits=[dt_ready[DBM - 1]], inc="Gy")
            if PART in ("prep", "dtmm", "dtrl", "dtact", "xtd"):
                chunk_bar = [proj_ev, ev_dtr, ev_bc, dt_ready[DBM - 1], y_init_ev[DBM - 1]]
                continue
            scan_ev = None
            last_evx = None
            yadd_last = None
            for s in range(S):
                s2 = s % 2
                evB = P.add("pe", lambda e, s=s, s2=s2: e.matmul(
                    pb[4 + s2][:, :], esel_s[:, s * 128:(s + 1) * 128], bc_s[:],
                    start=True, stop=True),
                    waits=[ev_bc, psB_war[s2], ev_flush], inc="Pe")
                evC = P.add("pe", lambda e, s=s, s2=s2: e.matmul(
                    pb[6 + s2][:, :], esel2_s[:, s * 128:(s + 1) * 128], bc_s[:],
                    start=True, stop=True), waits=[ev_bc, ev_flush, psC_war[s2]], inc="Pe")
                prev_evC_last = evC
                evCe = P.add("act", lambda e, s2=s2: e.copy(csbv(s2), pb[6 + s2][:, :]),
                             waits=[evC, csb_war[s2]], inc="Ac")
                psC_war[s2] = evCe
                for db in range(DBM):
                    i = s * 8 + db
                    ii = i % 4
                    evdA = P.add("act", lambda e, db=db, ii=ii, col=db * 16 + s: e.activation(
                        dAv(ii), dtv(db), AF.Exp, scale=A_s[:, col:col + 1]),
                        waits=[dt_ready[db], dA_war[ii]], inc="Ac")
                    evu = P.add("dve", lambda e, db=db, ii=ii, s2=s2: e.tensor_mul(
                        uv(ii), wv_(db), pb[4 + s2][:, :]), waits=[evB], inc="Dv")
                    evrl = P.add("act", lambda e, ii=ii: e.activation(
                        dbxv(ii), uv(ii), AF.Relu), waits=[evu, dbx_war[ii]], inc="Ac")
                    init = 0.0 if tau == 0 else state[:, i:i + 1]
                    evh = P.add("dve", lambda e, ii=ii, init=init: e.tensor_tensor_scan(
                        hv(ii), dAv(ii), dbxv(ii), init, ALU.mult, ALU.add),
                        waits=[evdA, evrl, h_war_pool[ii], h_war_act[ii]], inc="Dv")
                    scan_ev = evh
                    evx = P.add("act", lambda e, ii=ii, i=i: e.copy(
                        state[:, i:i + 1], hv(ii)[:, TC - 1:TC]), waits=[evh], inc="Ac")
                    last_evx = evx
                    P.add(SCAN_V_ENG, lambda e, ii=ii, s2=s2: e.tensor_mul(
                        vv(ii), hv(ii), csbv(s2)),
                        waits=[evh, evCe] + ([y_init_ev[db]] if s == 0 else []))
                    evy = P.add(SCAN_V_ENG, lambda e, ii=ii, db=db: e.tensor_add(
                        yv(db), yv(db), vv(ii)), inc="Po" if SCAN_V_ENG == "pool" else "Dv")
                    yadd_last = evy
                    dA_war[ii] = evh
                    h_war_pool[ii] = evy
                    h_war_act[ii] = evx
                    v_war[ii] = evy
                    dbx_war[ii] = evh
                psB_war[s2] = scan_ev
                csb_war[s2] = yadd_last
            yb_ev = P.add("sync", lambda e, tau=tau: e.dma_start(
                out=ybin[tau].ap().rearrange("(db p) t -> p db t", p=128),
                in_=a32[:, o_y:o_y + DBM * TC].rearrange("p (db t) -> p db t", db=DBM)),
                waits=[yadd_last], inc=f"Ly{tau}")
            ag_ev = P.add("pool", lambda e, tau=tau: e.collective_compute(
                "AllGather", ALU.bypass, replica_groups=PAIRS,
                ins=[ybin[tau].ap().opt()],
                outs=[ybag.ap()[tau * D:(tau + 1) * D, :].opt()]),
                waits=[yb_ev], inc="Cc")
            ag_evs.append(ag_ev)
            prev_cb2 = None
            chunk_bar = [scan_ev, yadd_last, yb_ev, proj_ev, ev_dtr, ev_bc, last_evx]

        if PART == "dtw":
            prev = ag_evs[-1]
            P.add("sync", lambda e: e.nop(), waits=[prev])
        if PART == "mixer":
            prev = ag_evs[-1]
            for q in range(2 * HG):
                eq = P.add("pool", lambda e, q=q: e.dma_start(
                    out=f32v(o_yf + (q % HG) * TC, TC),
                    in_=ybag.ap()[q * 128:(q + 1) * 128, :]),
                    waits=[prev], inc="Gy")
                prev = P.add("sync", lambda e, q=q: e.dma_start(
                    out=out[(q % HG) * 128:((q % HG) + 1) * 128, (q // HG) * TC:(q // HG + 1) * TC],
                    in_=f32v(o_yf + (q % HG) * TC, TC)), waits=[eq], inc="Lo0")

        all_ag = ag_evs[-1] if ag_evs else None
        slab_bar = []
        for sl in (range(2) if PART == "all" else ()):
            def yf_load(e, k=sl * 16):
                v = e.value_load(offs_s[0:1, k:k + 1])
                src = ybag.ap()[bass.ds(v, D), :].rearrange("(hg p) t -> p hg t", p=128)
                return e.dma_start(
                    out=a32[:, o_yf:o_yf + HG * TC].rearrange("p (hg t) -> p hg t", hg=HG),
                    in_=src)
            ev_yf = P.add("pool", yf_load,
                          waits=[all_ag, ev_flush] + slab_bar, inc="Gy")
            yf_evs = [ev_yf] * HG

            def emit_ln_stats(src_v, pre_evs, extra0):
                sw = [None, None]
                ev2 = None
                for hg in range(HG):
                    evq = P.add("act", lambda e, hg=hg: e.activation(
                        sqv(hg % 2), src_v(hg), AF.Square),
                        waits=[pre_evs[hg], sw[hg % 2]] + (extra0 if hg < 2 else []), inc="Ac")
                    P.add("pe", lambda e, hg=hg: e.matmul(
                        pb[0][0:1, :], one128_s[:], src_v(hg),
                        start=(hg == 0), stop=(hg == HG - 1)),
                        waits=[pre_evs[hg], ev_flush] + (extra0 if hg == 0 else []))
                    ev2 = P.add("pe", lambda e, hg=hg: e.matmul(
                        pb[1][0:1, :], one128_s[:], sqv(hg % 2),
                        start=(hg == 0), stop=(hg == HG - 1)),
                        waits=[evq], inc="Pe")
                    sw[hg % 2] = ev2
                return ev2, ev2

            def emit_ln_rows(stat_ev):
                P.add("dve", lambda e: e.tensor_scalar_mul(row(0), pb[0][0:1, :], INV_D),
                      waits=[stat_ev])
                P.add("dve", lambda e: e.tensor_scalar_mul(row(3), pb[1][0:1, :], INV_D))
                P.add("dve", lambda e: e.tensor_mul(row(2), row(0), row(0)))
                P.add("dve", lambda e: e.tensor_sub(row(3), row(3), row(2)))
                ev_var = P.add("dve", lambda e: e.tensor_scalar_add(row(3), row(3), EPS_LN),
                               inc="Dv")
                ev_ln = P.add("act", lambda e: e.activation(row(2), row(3), AF.Ln),
                              waits=[ev_var], inc="Ac")
                ev_hf = P.add("dve", lambda e: e.tensor_scalar_mul(row(2), row(2), -0.5),
                              waits=[ev_ln], inc="Dv")
                ev_rstd = P.add("act", lambda e: e.activation(row(1), row(2), AF.Exp),
                                waits=[ev_hf], inc="Ac")
                ev_b = P.add("dve", lambda e: e.scalar_tensor_tensor(
                    row(2), row(0), -1.0, row(1), ALU.mult, ALU.mult),
                    waits=[ev_rstd], inc="Dv")
                eva_r = P.add("pe", lambda e: e.matmul(pb[2][:, :], onec_s[:], row(1),
                                                       start=True, stop=True),
                              waits=[ev_b, ev_flush], inc="Pe")
                evb_r = P.add("pe", lambda e: e.matmul(pb[3][:, :], onec_s[:], row(2),
                                                       start=True, stop=True), inc="Pe")
                return ev_b, eva_r, evb_r

            st1, st2 = emit_ln_stats(yfv, yf_evs, slab_bar)
            ev_b, eva_r, evb_r = emit_ln_rows(st2)
            x1_evs = []
            for hg in range(HG):
                P.add("dve", lambda e, hg=hg: e.tensor_mul(yfv(hg), yfv(hg), pb[2][:, :]),
                      waits=[evb_r])
                P.add("dve", lambda e, hg=hg: e.tensor_add(yfv(hg), yfv(hg), pb[3][:, :]))
                x1_evs.append(P.add("dve", lambda e, hg=hg: e.tensor_copy(x1bv(hg), yfv(hg)),
                                    inc="Dv"))
            wgu_war = [None, None]
            gu_done = [None, None]
            ft_war = [None, None]
            gt_war = [None, None]
            fdma_evs = []
            for it in range(NIT):
                ii = it % 2
                evw = P.add("pool", lambda e, it=it, ii=ii: e.dma_start(
                    out=b16v(o_wgu[ii], 4096), in_=hgu[it]), waits=[wgu_war[ii]], inc=f"Gw{ii}")
                ba, bb = 4 + 2 * ii, 5 + 2 * ii
                evg = evu = None
                for kg in range(DBA):
                    st, sp = kg == 0, kg == DBA - 1
                    evg = P.add("pe", lambda e, kg=kg, st=st, sp=sp, ba=ba, ii=ii: e.matmul(
                        pb[ba][:, :], wguv(ii, kg * 128, 128), x1bv(kg), start=st, stop=sp),
                        waits=[evw, x1_evs[kg]] + ([gu_done[ii]] if st else []),
                        inc="Pe" if sp else None) or evg
                    evu = P.add("pe", lambda e, kg=kg, st=st, sp=sp, bb=bb, ii=ii: e.matmul(
                        pb[bb][:, :], wguv(ii, 2048 + kg * 128, 128), x1bv(kg),
                        start=st, stop=sp), inc="Pe" if sp else None) or evu
                wgu_war[ii] = evu
                evgt = P.add("act", lambda e, it=it, ii=ii, ba=ba: e.activation(
                    gtv(ii), pb[ba][:, :], AF.Silu, scale=sg_s[:, it:it + 1]),
                    waits=[evg, evu, gt_war[ii]], inc="Ac")
                evft = P.add("dve", lambda e, it=it, ii=ii, bb=bb: e.scalar_tensor_tensor(
                    ftv(ii), pb[bb][:, :], su_s[:, it:it + 1], gtv(ii), ALU.mult, ALU.mult),
                    waits=[evgt, ft_war[ii]], inc="Dv")
                gu_done[ii] = evft
                gt_war[ii] = evft
                evfd = P.add("sync", lambda e, it=it, ii=ii: e.dma_start(
                    out=fuse_d[it], in_=ftv(ii)), waits=[evft], inc=f"Lf{ii}")
                fdma_evs.append(evfd)
                ft_war[ii] = evfd
            ev_evac_all = [None] * HG
            prev_evac = []
            wd_war = [None] * 4
            frd_war = [None] * 4
            for hh in range(2):
                down_done = [None] * 8
                evd = None
                for it in range(NIT):
                    ii = it % 4
                    evwd = P.add("pool", lambda e, it=it, ii=ii, hh=hh: e.dma_start(
                        out=b16v(o_wd[ii], 1024), in_=hd[it, :, hh * 1024:(hh + 1) * 1024]),
                        waits=[wd_war[ii]], inc=f"Gv{ii}")
                    evfr = P.add("sync", lambda e, it=it, ii=ii: e.dma_start(
                        out=frdv(ii), in_=fuse_d[it]),
                        waits=[fdma_evs[-2], fdma_evs[-1], frd_war[ii]], inc=f"Lr{ii}")
                    for g8 in range(8):
                        st, sp = it == 0, it == NIT - 1
                        w0 = [evwd, evfr]
                        if st:
                            w0 = w0 + [ev_b, x1_evs[-1]] + prev_evac
                            if hh == 0:
                                w0 = w0 + [gu_done[0], gu_done[1]]
                        evd = P.add("pe", lambda e, ii=ii, g8=g8, st=st, sp=sp: e.matmul(
                            pb[g8][:, :], wdv(ii, g8 * 128, 128), frdv(ii),
                            start=st, stop=sp), waits=w0,
                            inc="Pe" if (g8 == 7 or sp) else None) or evd
                        if sp:
                            down_done[g8] = evd
                    wd_war[ii] = evd
                    frd_war[ii] = evd
                cur = []
                for g8 in range(8):
                    hg = hh * 8 + g8
                    eve = P.add("dve", lambda e, g8=g8, hg=hg: e.scalar_tensor_tensor(
                        y2v(hg), pb[g8][:, :], sd_s[:, hg:hg + 1], yfv(hg),
                        ALU.mult, ALU.add), waits=[down_done[g8]], inc="Dv")
                    cur.append(eve)
                    ev_evac_all[hg] = eve
                prev_evac = cur
            st1, st2 = emit_ln_stats(y2v, ev_evac_all, ev_evac_all)
            ev_b2, eva_r2, evb_r2 = emit_ln_rows(st2)
            ost_war = [None, None]
            slab_bar = []
            for hg in range(HG):
                P.add("dve", lambda e, hg=hg: e.tensor_mul(ostv(hg % 2), y2v(hg), pb[2][:, :]),
                      waits=[evb_r2, ost_war[hg % 2]])
                evo = P.add("dve", lambda e, hg=hg: e.tensor_add(
                    ostv(hg % 2), ostv(hg % 2), pb[3][:, :]), inc="Dv")
                evod = P.add("sync", lambda e, hg=hg, sl=sl: e.dma_start(
                    out=out[hg * 128:(hg + 1) * 128, sl * TC:(sl + 1) * TC], in_=ostv(hg % 2)),
                    waits=[evo], inc=f"Lo{hg % 2}")
                ost_war[hg % 2] = evod
                slab_bar.append(evod)
            slab_bar.append(ev_evac_all[-1])
            slab_bar.append(st2)

        P.add("sync", lambda e: e.nop(),
              waits=[(s, P.cnt[s]) for s in list(P.cnt) if is_dma_sem(s)])

        with nc.Block() as block:
            @block.sync
            def _(eng):
                P.emit_engine("sync", eng, sems)

            @block.tensor
            def _(eng):
                P.emit_engine("pe", eng, sems)

            @block.scalar
            def _(eng):
                P.emit_engine("act", eng, sems)

            @block.vector
            def _(eng):
                P.emit_engine("dve", eng, sems)

            @block.gpsimd
            def _(eng):
                P.emit_engine("pool", eng, sems)

    return nc


_NC_CACHE = {}


def _get_nc():
    if "nc" not in _NC_CACHE:
        _NC_CACHE["nc"] = build_nc()
    return _NC_CACHE["nc"]


def kernel(L0_attn_x, L0_attn_conv_w, L0_attn_conv_b, L0_attn_x_proj_w,
           L0_attn_dt_proj_w, L0_attn_dt_proj_b, L0_attn_A_log, L0_attn_D_param,
           L0_ffn_w_gate, L0_ffn_w_up, L0_ffn_w_down, L0_ffn_s_gate,
           L0_ffn_s_up, L0_ffn_s_down):
    x = np.asarray(L0_attn_x, np.float32)
    conv_w = np.asarray(L0_attn_conv_w, np.float32)
    conv_b = np.asarray(L0_attn_conv_b, np.float32)
    x_proj_w = np.asarray(L0_attn_x_proj_w, np.float32)
    dt_proj_w = np.asarray(L0_attn_dt_proj_w, np.float32)
    dt_proj_b = np.asarray(L0_attn_dt_proj_b, np.float32)
    A_log = np.asarray(L0_attn_A_log, np.float32)
    D_param = np.asarray(L0_attn_D_param, np.float32)
    wg8 = np.asarray(L0_ffn_w_gate)
    wu8 = np.asarray(L0_ffn_w_up)
    wd8 = np.asarray(L0_ffn_w_down)
    s_gate = np.asarray(L0_ffn_s_gate, np.float32)
    s_up = np.asarray(L0_ffn_s_up, np.float32)
    s_down = np.asarray(L0_ffn_s_down, np.float32)

    def half_prep(role):
        lo = role * 1024
        perm = np.concatenate([np.arange(lo, lo + 1024),
                               np.arange((1 - role) * 1024, (1 - role) * 1024 + 1024)])
        d = {}
        d["cw"] = np.ascontiguousarray(
            conv_w[perm, 0, :].reshape(16, 128, 4).transpose(1, 0, 2).reshape(128, 64))
        d["cbi"] = np.ascontiguousarray(conv_b[perm].reshape(16, 128).T)
        d["dpi"] = np.ascontiguousarray(D_param[lo:lo + 1024].reshape(8, 128).T)
        d["dtbi"] = np.ascontiguousarray(dt_proj_b[lo:lo + 1024].reshape(8, 128).T)
        d["alog"] = np.ascontiguousarray(
            A_log[lo:lo + 1024].reshape(8, 128, 16).transpose(1, 0, 2).reshape(128, 128))
        d["wxp"] = np.ascontiguousarray(x_proj_w.T[perm])
        d["wdt"] = np.ascontiguousarray(dt_proj_w[lo:lo + 1024].T)
        off = np.zeros((1, 32), np.uint32)
        for sl in range(2):
            for hg in range(HG):
                off[0, sl * 16 + hg] = (role * 2 + sl) * D + hg * 128
        d["offs"] = off
        return d, perm

    halves = [half_prep(0), half_prep(1)]

    esel = np.zeros((32, 2048), np.float32)
    esel2 = np.zeros((32, 2048), np.float32)
    for s in range(16):
        esel[s, s * 128:(s + 1) * 128] = 1.0
        esel2[16 + s, s * 128:(s + 1) * 128] = 1.0
    onescol = np.ones((1, 128), np.float32)
    ones128 = np.ones((128, 1), np.float32)
    g4 = wg8.reshape(16, 128, 64, 128).transpose(2, 1, 0, 3)
    u4 = wu8.reshape(16, 128, 64, 128).transpose(2, 1, 0, 3)
    hgu = np.ascontiguousarray(np.stack([g4, u4], axis=2).reshape(NIT, 128, 4096))
    hd = np.ascontiguousarray(wd8.reshape(NIT, 128, 2048))
    sgl = np.ascontiguousarray(s_gate.reshape(64, 128).T)
    sul = np.ascontiguousarray(s_up.reshape(64, 128).T)
    sdl = np.ascontiguousarray(s_down.reshape(16, 128).T)

    in_maps = []
    for c in range(8):
        b, role = c // 2, c % 2
        d, perm = halves[role]
        xTh = np.zeros((D, T + 3), np.float32)
        xTh[:, 3:] = x[b].T[perm]
        m = {"xT": xTh, "esel": esel, "esel2": esel2, "onescol": onescol, "ones128": ones128,
             "hgu": hgu, "hd": hd, "sgd": sgl, "sud": sul, "sdd": sdl}
        m.update(d)
        in_maps.append(m)

    nc = _get_nc()
    res = run_bass_kernel_spmd(nc, in_maps, core_ids=list(range(8)))

    result = np.empty((BATCH, T, D), np.float32)
    for c in range(8):
        b, role = c // 2, c % 2
        o = np.asarray(res.results[c]["out"])
        result[b, role * 1024:(role + 1) * 1024, :] = o.T
    return result




# version 08 (final)
# speedup vs baseline: 1.1462x; 1.0681x over previous
import contextlib
import numpy as np
import ml_dtypes
import concourse.bass as bass
import concourse.mybir as mybir
from concourse.bass_utils import run_bass_kernel_spmd

F32 = mybir.dt.float32
BF16 = mybir.dt.bfloat16
I8 = mybir.dt.int8
U32 = mybir.dt.uint32
AF = mybir.ActivationFunctionType
ALU = mybir.AluOpType

D = 2048
T = 2048
BATCH = 4
TC = 512
NCH = T // TC
DBA = 16
DBM = 8
S = 16
RK = 128
NIT = 64
HG = 16
INV_D = 1.0 / D
EPS_LN = 1e-5
PAIRS = [[0, 1], [2, 3], [4, 5], [6, 7]]

ENGINES = ("sync", "pe", "act", "dve", "pool")
OWN_SEM = {"pe": "Pe", "act": "Ac", "dve": "Dv", "pool": "Po"}
def is_dma_sem(s):
    return s[0] in ("L", "G")


class Prog:

    def __init__(self):
        self.ops = []
        self.cnt = {}

    def ev_last_dv(self):
        return ("Dv", self.cnt.get("Dv", 0))

    def add(self, eng, fn, waits=(), inc=None):
        ev = None
        if inc is not None:
            self.cnt[inc] = self.cnt.get(inc, 0) + 1
            ev = (inc, self.cnt[inc])
        self.ops.append((eng, fn, tuple(w for w in waits if w is not None), inc))
        return ev

    def emit_engine(self, name, eng, sems):
        known = {}
        own = OWN_SEM.get(name)
        for oeng, fn, waits, inc in self.ops:
            if oeng != name:
                continue
            for sem, val in waits:
                if sem.endswith("!"):
                    sem = sem[:-1]
                elif sem == own:
                    continue
                tgt = val * 16 if is_dma_sem(sem) else val
                if known.get(sem, 0) < tgt:
                    eng.wait_ge(sems[sem], tgt)
                    known[sem] = tgt
            ins = fn(eng)
            if inc is not None:
                ins.then_inc(sems[inc], 16 if is_dma_sem(inc) else 1)


PART = "all"
SCAN_V_ENG = "pool"


def build_nc():
    nc = bass.Bass("TRN2", target_bir_lowering=False, debug=False)

    xT = nc.dram_tensor("xT", [D, T + 3], F32, kind="ExternalInput").ap()
    cw = nc.dram_tensor("cw", [128, 64], F32, kind="ExternalInput").ap()
    cbi = nc.dram_tensor("cbi", [128, 16], F32, kind="ExternalInput").ap()
    dpi = nc.dram_tensor("dpi", [128, 8], F32, kind="ExternalInput").ap()
    dtbi = nc.dram_tensor("dtbi", [128, 8], F32, kind="ExternalInput").ap()
    alog = nc.dram_tensor("alog", [128, 128], F32, kind="ExternalInput").ap()
    wxp = nc.dram_tensor("wxp", [D, 160], F32, kind="ExternalInput").ap()
    wdt = nc.dram_tensor("wdt", [RK, 1024], F32, kind="ExternalInput").ap()
    esel = nc.dram_tensor("esel", [32, 2048], F32, kind="ExternalInput").ap()
    esel2 = nc.dram_tensor("esel2", [32, 2048], F32, kind="ExternalInput").ap()
    onescol = nc.dram_tensor("onescol", [1, 128], F32, kind="ExternalInput").ap()
    ones128 = nc.dram_tensor("ones128", [128, 1], F32, kind="ExternalInput").ap()
    hgu = nc.dram_tensor("hgu", [NIT, 128, 4096], I8, kind="ExternalInput").ap()
    hd = nc.dram_tensor("hd", [NIT, 128, 2048], I8, kind="ExternalInput").ap()
    sgd = nc.dram_tensor("sgd", [128, 64], F32, kind="ExternalInput").ap()
    sud = nc.dram_tensor("sud", [128, 64], F32, kind="ExternalInput").ap()
    sdd = nc.dram_tensor("sdd", [128, 16], F32, kind="ExternalInput").ap()
    offs = nc.dram_tensor("offs", [1, 32], U32, kind="ExternalInput").ap()

    out = nc.dram_tensor("out", [D, 1024], F32, kind="ExternalOutput").ap()

    ybin = [nc.dram_tensor(f"ybin{t}", [1024, TC], F32) for t in range(NCH)]
    ybag = nc.dram_tensor("ybag", [NCH * D, TC], F32)
    fuse_d = nc.dram_tensor("fuse_d", [NIT, 128, TC], BF16)

    ctx = contextlib.ExitStack()
    sb = lambda name, shape, dt: ctx.enter_context(nc.sbuf_tensor(name, shape, dt))
    psb = lambda name, shape: ctx.enter_context(nc.psum_tensor(name, shape, F32))

    with ctx:
        cw_s = sb("cw_s", [128, 64], F32)
        cb_s = sb("cb_s", [128, 16], F32)
        dp_s = sb("dp_s", [128, 8], F32)
        dtb_s = sb("dtb_s", [128, 8], F32)
        A_s = sb("A_s", [128, 128], F32)
        Atmp_s = sb("Atmp_s", [128, 128], F32)
        flsh_s = sb("flsh_s", [128, 128], F32)
        wxp_s = sb("wxp_s", [128, 16 * 160], BF16)
        wdt_s = sb("wdt_s", [128, 1024], BF16)
        esel_s = sb("esel_s", [32, 2048], F32)
        esel2_s = sb("esel2_s", [32, 2048], F32)
        onec_s = sb("onec_s", [1, 128], F32)
        one128_s = sb("one128_s", [128, 1], F32)
        sg_s = sb("sg_s", [128, 64], F32)
        su_s = sb("su_s", [128, 64], F32)
        sd_s = sb("sd_s", [128, 16], F32)
        offs_s = sb("offs_s", [1, 32], U32)
        state = sb("state", [128, 128], F32)
        bc_s = sb("bc_s", [32, TC], F32)
        dtr_s = sb("dtr_s", [128, TC], BF16)

        AB = 158 * 1024
        arena = sb("arena", [128, AB], I8)
        a32 = arena.bitcast(F32)
        a16 = arena.bitcast(BF16)

        class Alloc:
            def __init__(self):
                self.cur = 0

            def f32(self, nelem):
                off = (self.cur + 3) // 4 * 4
                self.cur = off + nelem * 4
                assert self.cur <= AB, f"arena overflow {self.cur}"
                return off // 4

            def bf16(self, nelem):
                off = (self.cur + 1) // 2 * 2
                self.cur = off + nelem * 2
                assert self.cur <= AB, f"arena overflow {self.cur}"
                return off // 2

        M = Alloc()
        o_xt = [M.f32(TC + 3) for _ in range(4)]
        o_acc = [M.f32(TC) for _ in range(2)]
        o_acc2 = [M.f32(TC) for _ in range(2)]
        o_xcm = M.f32(DBM * TC)
        o_xcb = M.bf16(DBA * TC)
        o_dt = M.f32(DBM * TC)
        o_w = M.f32(DBM * TC)
        o_y = M.f32(DBM * TC)
        o_dA = [M.f32(TC) for _ in range(4)]
        o_dbx = [M.f32(TC) for _ in range(4)]
        o_u = [M.f32(TC) for _ in range(4)]
        o_h = [M.f32(TC) for _ in range(4)]
        o_v = [M.f32(TC) for _ in range(4)]
        o_csb = [M.f32(TC) for _ in range(2)]
        o_sp0 = [M.f32(TC) for _ in range(2)]
        o_sp1 = [M.f32(TC) for _ in range(2)]
        o_rl = [M.f32(TC) for _ in range(2)]
        o_usb = [M.f32(TC) for _ in range(2)]

        F = Alloc()
        o_yf = F.f32(HG * TC)
        o_y2 = F.f32(HG * TC)
        o_x1b = F.bf16(HG * TC)
        o_wgu = [F.bf16(4096) for _ in range(2)]
        o_wd = [F.bf16(1024) for _ in range(4)]
        o_frd = [F.bf16(TC) for _ in range(4)]
        o_gt = [F.f32(TC) for _ in range(2)]
        o_ft = [F.bf16(TC) for _ in range(2)]
        o_sq = [F.f32(TC) for _ in range(2)]
        o_rows = F.f32(4 * TC)
        o_ost = [F.f32(TC) for _ in range(2)]

        f32v = lambda off, n, p0=0, p1=128: a32[p0:p1, off:off + n]
        b16v = lambda off, n, p0=0, p1=128: a16[p0:p1, off:off + n]

        xtv = lambda i: f32v(o_xt[i], TC + 3)
        xts = lambda i, a, n: f32v(o_xt[i] + a, n)
        accv = lambda i: f32v(o_acc[i], TC)
        acc2v = lambda i: f32v(o_acc2[i], TC)
        xcm = lambda db: f32v(o_xcm + db * TC, TC)
        xcb = lambda db: b16v(o_xcb + db * TC, TC)
        dtv = lambda db: f32v(o_dt + db * TC, TC)
        wv_ = lambda db: f32v(o_w + db * TC, TC)
        yv = lambda db: f32v(o_y + db * TC, TC)
        dAv = lambda i: f32v(o_dA[i], TC)
        dbxv = lambda i: f32v(o_dbx[i], TC)
        uv = lambda i: f32v(o_u[i], TC)
        hv = lambda i: f32v(o_h[i], TC)
        vv = lambda i: f32v(o_v[i], TC)
        csbv = lambda i: f32v(o_csb[i], TC)
        sp0v = lambda i: f32v(o_sp0[i], TC)
        sp1v = lambda i: f32v(o_sp1[i], TC)
        rlv = lambda i: f32v(o_rl[i], TC)
        usbv = lambda i: f32v(o_usb[i], TC)

        yfv = lambda hg: f32v(o_yf + hg * TC, TC)
        y2v = lambda hg: f32v(o_y2 + hg * TC, TC)
        x1bv = lambda hg: b16v(o_x1b + hg * TC, TC)
        wguv = lambda i, a, n: b16v(o_wgu[i] + a, n)
        wdv = lambda i, a, n: b16v(o_wd[i] + a, n)
        frdv = lambda i: b16v(o_frd[i], TC)
        gtv = lambda i: f32v(o_gt[i], TC)
        ftv = lambda i: b16v(o_ft[i], TC)
        sqv = lambda i: f32v(o_sq[i], TC)
        row = lambda i: a32[0:1, o_rows + i * TC: o_rows + (i + 1) * TC]
        ostv = lambda i: f32v(o_ost[i], TC)

        pb = [psb(f"pb{i}", [128, 512]) for i in range(8)]

        sem_names = (["Lc", "Gc", "Gy", "Lf0", "Lf1", "Cc", "Pe", "Ac", "Dv", "Po"]
                     + [f"Lx{i}" for i in range(4)] + [f"Ly{i}" for i in range(NCH)]
                     + [f"Lr{i}" for i in range(4)] + [f"Lo{i}" for i in range(2)]
                     + [f"Gw{i}" for i in range(2)] + [f"Gv{i}" for i in range(4)])
        sems = {n: ctx.enter_context(nc.semaphore(n)) for n in sem_names}

        P = Prog()

        const_loads = [("cw", cw_s, cw), ("cb", cb_s, cbi), ("dp", dp_s, dpi),
                       ("dtb", dtb_s, dtbi), ("alog", Atmp_s, alog),
                       ("esel", esel_s, esel), ("esel2", esel2_s, esel2), ("onec", onec_s, onescol),
                       ("one128", one128_s, ones128), ("sg", sg_s, sgd),
                       ("su", su_s, sud), ("sd", sd_s, sdd), ("offs", offs_s, offs)]
        const_evs = {}
        for name, dst, src in const_loads:
            const_evs[name] = P.add("sync", lambda e, d=dst, s=src: e.dma_start(out=d[:], in_=s), inc="Lc")
        ev_wxp = P.add("pool", lambda e: e.dma_start(
            out=wxp_s.rearrange("p (kg f) -> p kg f", kg=16),
            in_=wxp.rearrange("(kg p) f -> p kg f", p=128)), inc="Gc")
        ev_wdt = P.add("pool", lambda e: e.dma_start(out=wdt_s[:], in_=wdt), inc="Gc")
        ev_flush = P.add("sync", lambda e: e.dma_start(out=flsh_s[:], in_=alog), inc="Lc")
        evA1 = P.add("act", lambda e: e.activation(A_s[:], Atmp_s[:], AF.Exp),
                     waits=[ev_flush, const_evs["alog"]], inc="Ac")
        evA = P.add("dve", lambda e: e.tensor_scalar_mul(A_s[:], A_s[:], -1.0),
                    waits=[evA1], inc="Dv")

        xt_war = [None] * 4
        acc_war = [None] * 2
        dA_war = [None] * 4
        h_war_pool = [None] * 4
        h_war_act = [None] * 4
        v_war = [None] * 4
        dbx_war = [None] * 4
        u_war = [None] * 4
        csb_war = [None] * 2
        sp_war = [None] * 2
        psB_war = [None] * 2
        psC_war = [None] * 2
        psdt_war = [None] * 2
        ag_evs = []
        dt_ready = [None] * DBM
        y_init_ev = [None] * DBM
        chunk_bar = []
        prev_cb2 = None
        prev_evC_last = None

        for tau in range(NCH):
            t0 = tau * TC
            xc_ready = [None] * DBA
            for db in range(DBA):
                it4 = db % 4
                ld_ev = P.add("sync", lambda e, i=it4, r0=db * 128, c0=t0: e.dma_start(
                    out=xtv(i), in_=xT[r0:r0 + 128, c0:c0 + TC + 3]),
                    waits=[xt_war[it4]] + (chunk_bar if db < 4 else []), inc=f"Lx{it4}")
                i2 = db % 2
                P.add("dve", lambda e, i2=i2, i=it4, db=db: e.tensor_scalar(
                    accv(i2), xts(i, 0, TC), cw_s[:, db * 4:db * 4 + 1], None, ALU.mult),
                    waits=[ld_ev, acc_war[i2], evA] + (chunk_bar if db < 2 else []))
                P.add("dve", lambda e, i2=i2, i=it4, db=db: e.scalar_tensor_tensor(
                    acc2v(i2), xts(i, 1, TC), cw_s[:, db * 4 + 1:db * 4 + 2],
                    accv(i2), ALU.mult, ALU.add))
                P.add("dve", lambda e, i2=i2, i=it4, db=db: e.scalar_tensor_tensor(
                    accv(i2), xts(i, 2, TC), cw_s[:, db * 4 + 2:db * 4 + 3],
                    acc2v(i2), ALU.mult, ALU.add))
                tap_ev = P.add("dve", lambda e, i2=i2, i=it4, db=db: e.scalar_tensor_tensor(
                    acc2v(i2), xts(i, 3, TC), cw_s[:, db * 4 + 3:db * 4 + 4],
                    accv(i2), ALU.mult, ALU.add), inc="Dv")
                if PART == "xtd" and tau == 1 and db < 4:
                    xd_ev = P.add("sync", lambda e, i=it4, db=db: e.dma_start(
                        out=out[db * 128:(db + 1) * 128, 0:TC + 3], in_=xtv(i)),
                        waits=[tap_ev], inc=f"Ly{tau % NCH}")
                    tap_ev = xd_ev
                if db < DBM:
                    sl_ev = P.add("act", lambda e, i2=i2, db=db: e.activation(
                        xcm(db), acc2v(i2), AF.Silu, bias=cb_s[:, db:db + 1]),
                        waits=[tap_ev] + (chunk_bar if db < 2 else []), inc="Ac")
                    cp_ev = P.add("dve", lambda e, db=db: e.tensor_copy(xcb(db), xcm(db)),
                                  waits=[sl_ev], inc="Dv")
                    yi_ev = P.add("dve", lambda e, db=db, i=it4: e.scalar_tensor_tensor(
                        yv(db), xcm(db), dp_s[:, db:db + 1], xts(i, 3, TC),
                        ALU.mult, ALU.add), inc="Dv")
                    y_init_ev[db] = yi_ev
                    xc_ready[db] = cp_ev
                    xt_war[it4] = yi_ev
                    acc_war[i2] = cp_ev
                else:
                    sl_ev = P.add("act", lambda e, i2=i2, db=db: e.activation(
                        xcb(db), acc2v(i2), AF.Silu, bias=cb_s[:, db:db + 1]),
                        waits=[tap_ev], inc="Ac")
                    xc_ready[db] = sl_ev
                    xt_war[it4] = tap_ev
                    acc_war[i2] = sl_ev
            if PART == "conv":
                chunk_bar = [xc_ready[DBA - 1], y_init_ev[DBM - 1]]
                continue
            proj_ev = None
            for kg in range(DBA):
                st, sp = kg == 0, kg == DBA - 1
                P.add("pe", lambda e, kg=kg, st=st, sp=sp: e.matmul(
                    pb[0][:, :], wxp_s[:, kg * 160:kg * 160 + 128], xcb(kg), start=st, stop=sp),
                    waits=[xc_ready[kg], ev_wdt] + (chunk_bar if st else []),
                    inc="Pe" if sp else None)
                proj_ev = P.add("pe", lambda e, kg=kg, st=st, sp=sp: e.matmul(
                    pb[1][0:32, :], wxp_s[:, kg * 160 + 128:kg * 160 + 160], xcb(kg),
                    start=st, stop=sp), inc="Pe" if sp else None) or proj_ev
            ev_dtr = P.add("act", lambda e: e.copy(dtr_s[:], pb[0][:, :]),
                           waits=[proj_ev], inc="Ac")
            ev_bc = P.add("act", lambda e: e.copy(bc_s[:], pb[1][0:32, :]),
                          waits=[proj_ev, prev_cb2, prev_evC_last], inc="Ac")
            if PART == "proj":
                chunk_bar = [proj_ev, ev_dtr, ev_bc, y_init_ev[DBM - 1]]
                continue
            for db in range(DBM):
                i2 = db % 2
                mm_ev = P.add("pe", lambda e, db=db, i2=i2: e.matmul(
                    pb[2 + i2][:, :], wdt_s[:, db * 128:(db + 1) * 128], dtr_s[:],
                    start=True, stop=True),
                    waits=[ev_dtr, ev_wdt, psdt_war[i2]], inc="Pe")
                if PART == "dtmm":
                    dt_ev = P.add("act", lambda e, db=db, i2=i2: e.copy(
                        dtv(db), pb[2 + i2][:, :]), waits=[mm_ev, sp_war[i2]], inc="Ac")
                    P.add("dve", lambda e, db=db: e.tensor_mul(wv_(db), dtv(db), xcm(db)),
                          waits=[dt_ev], inc="Dv")
                    dt_ready[db] = dt_ev
                    sp_war[i2] = dt_ev
                    psdt_war[i2] = dt_ev
                    continue
                if PART == "dtrl":
                    rl_ev = P.add("dve", lambda e, db=db, i2=i2: e.tensor_scalar(
                        rlv(i2), pb[2 + i2][:, :], dtb_s[:, db:db + 1], 0.0,
                        ALU.add, ALU.max), waits=[mm_ev, sp_war[i2]])
                    dt_ev = P.add("dve", lambda e, db=db, i2=i2: e.tensor_copy(
                        dtv(db), rlv(i2)), inc="Dv")
                    P.add("dve", lambda e, db=db: e.tensor_mul(wv_(db), dtv(db), xcm(db)),
                          inc="Dv")
                    dt_ready[db] = dt_ev
                    sp_war[i2] = dt_ev
                    psdt_war[i2] = dt_ev
                    continue
                if PART == "dtact":
                    P.add("act", lambda e, db=db, i2=i2: e.activation(
                        sp0v(i2), pb[2 + i2][:, :], AF.Abs, bias=dtb_s[:, db:db + 1]),
                        waits=[mm_ev, sp_war[i2]])
                    P.add("act", lambda e, i2=i2: e.activation(sp1v(i2), sp0v(i2), AF.Exp, scale=-1.0))
                    ab_ev = P.add("act", lambda e, i2=i2: e.activation(
                        sp0v(i2), sp1v(i2), AF.Ln, bias=1.0), inc="Ac")
                    dt_ev = P.add("dve", lambda e, db=db, i2=i2: e.tensor_copy(
                        dtv(db), sp0v(i2)), waits=[ab_ev], inc="Dv")
                    P.add("dve", lambda e, db=db: e.tensor_mul(wv_(db), dtv(db), xcm(db)),
                          inc="Dv")
                    dt_ready[db] = dt_ev
                    sp_war[i2] = dt_ev
                    psdt_war[i2] = dt_ev
                    continue
                ev_usb = P.add("dve", lambda e, db=db, i2=i2: e.tensor_scalar_add(
                    usbv(i2), pb[2 + i2][:, :], dtb_s[:, db:db + 1]),
                    waits=[mm_ev, sp_war[i2]], inc="Dv")
                P.add("dve", lambda e, i2=i2: e.tensor_scalar_max(
                    rlv(i2), usbv(i2), 0.0))
                P.add("act", lambda e, i2=i2: e.activation(sp0v(i2), usbv(i2), AF.Abs),
                      waits=[ev_usb, sp_war[i2]])
                P.add("act", lambda e, i2=i2: e.activation(sp1v(i2), sp0v(i2), AF.Exp, scale=-1.0))
                ab_ev = P.add("act", lambda e, i2=i2: e.activation(
                    sp0v(i2), sp1v(i2), AF.Ln, bias=1.0), inc="Ac")
                dt_ev = P.add("dve", lambda e, db=db, i2=i2: e.tensor_add(
                    dtv(db), rlv(i2), sp0v(i2)), waits=[ab_ev], inc="Dv")
                P.add("dve", lambda e, db=db: e.tensor_mul(wv_(db), dtv(db), xcm(db)), inc="Dv")
                dt_ready[db] = dt_ev
                sp_war[i2] = dt_ev
                psdt_war[i2] = ev_usb
            if PART in ("dtw", "prep") and tau < 2:
                P.add("sync", lambda e, tau=tau: e.dma_start(
                    out=out[0:1024, tau * TC:(tau + 1) * TC].rearrange("(db p) t -> p db t", p=128),
                    in_=a32[:, o_xcm:o_xcm + DBM * TC].rearrange("p (db t) -> p db t", db=DBM)),
                    waits=[dt_ready[DBM - 1], P.ev_last_dv()], inc=f"Ly{tau}")
                P.add("pool", lambda e, tau=tau: e.dma_start(
                    out=out[1024:2048, tau * TC:(tau + 1) * TC].rearrange("(db p) t -> p db t", p=128),
                    in_=a16[:, o_xcb:o_xcb + DBM * TC].rearrange("p (db t) -> p db t", db=DBM)),
                    waits=[dt_ready[DBM - 1]], inc="Gy")
            if PART in ("prep", "dtmm", "dtrl", "dtact", "xtd"):
                chunk_bar = [proj_ev, ev_dtr, ev_bc, dt_ready[DBM - 1], y_init_ev[DBM - 1]]
                continue
            scan_ev = None
            last_evx = None
            yadd_last = None
            NI = S * DBM
            evB_s = [None] * S
            evCe_s = [None] * S
            rl_evs = [None] * NI
            dA_evs = [None] * NI
            u_evs = [None] * NI

            def emit_selmm(s):
                s2 = s % 2
                evB_s[s] = P.add("pe", lambda e, s=s, s2=s2: e.matmul(
                    pb[4 + s2][:, :], esel_s[:, s * 128:(s + 1) * 128], bc_s[:],
                    start=True, stop=True),
                    waits=[ev_bc, psB_war[s2], ev_flush], inc="Pe")
                evC = P.add("pe", lambda e, s=s, s2=s2: e.matmul(
                    pb[6 + s2][:, :], esel2_s[:, s * 128:(s + 1) * 128], bc_s[:],
                    start=True, stop=True), waits=[ev_bc, ev_flush, psC_war[s2]], inc="Pe")
                evCe_s[s] = P.add("act", lambda e, s2=s2: e.copy(csbv(s2), pb[6 + s2][:, :]),
                                  waits=[evC, csb_war[s2]], inc="Ac")
                psC_war[s2] = evCe_s[s]

            def emit_front(j):
                s, db = j // DBM, j % DBM
                jj = j % 4
                dA_evs[j] = P.add("act", lambda e, db=db, jj=jj, col=db * 16 + s: e.activation(
                    dAv(jj), dtv(db), AF.Exp, scale=A_s[:, col:col + 1]),
                    waits=[dt_ready[db], dA_war[jj]], inc="Ac")
                u_evs[j] = P.add("dve", lambda e, db=db, jj=jj, s2=s % 2: e.tensor_mul(
                    uv(jj), wv_(db), pb[4 + s2][:, :]),
                    waits=[evB_s[s], u_war[jj]], inc="Dv")
                evu = u_evs[j]
                rl_evs[j] = P.add("act", lambda e, jj=jj: e.activation(
                    dbxv(jj), uv(jj), AF.Relu), waits=[evu, dbx_war[jj]], inc="Ac")

            emit_selmm(0)
            emit_selmm(1)
            emit_front(0)
            for i in range(NI):
                s, db = i // DBM, i % DBM
                ii = i % 4
                if i + 1 < NI:
                    emit_front(i + 1)
                init = 0.0 if tau == 0 else state[:, i:i + 1]
                evh = P.add("dve", lambda e, ii=ii, init=init: e.tensor_tensor_scan(
                    hv(ii), dAv(ii), dbxv(ii), init, ALU.mult, ALU.add),
                    waits=[dA_evs[i], rl_evs[i], h_war_pool[ii], h_war_act[ii]], inc="Dv")
                scan_ev = evh
                evx = P.add("act", lambda e, ii=ii, i=i: e.copy(
                    state[:, i:i + 1], hv(ii)[:, TC - 1:TC]), waits=[evh], inc="Ac")
                last_evx = evx
                veng = "dve" if (i % 2 == 0) else "pool"
                evv = P.add(veng, lambda e, ii=ii, s2=s % 2: e.tensor_mul(
                    vv(ii), hv(ii), csbv(s2)),
                    waits=[evh, evCe_s[s], v_war[ii]] + ([y_init_ev[db]] if s == 0 else []),
                    inc="Dv" if veng == "dve" else None)
                evy = P.add("pool", lambda e, ii=ii, db=db: e.tensor_add(
                    yv(db), yv(db), vv(ii)), waits=([evv] if veng == "dve" else []),
                    inc="Po")
                yadd_last = evy
                dA_war[ii] = evh
                h_war_pool[ii] = evy
                h_war_act[ii] = evx
                v_war[ii] = evy
                dbx_war[ii] = evh
                u_war[ii] = rl_evs[i]
                if db == DBM - 1:
                    psB_war[s % 2] = u_evs[i]
                    csb_war[s % 2] = yadd_last
                    if s + 2 < S:
                        emit_selmm(s + 2)
            prev_evC_last = psC_war[(S - 1) % 2]
            yb_ev = P.add("sync", lambda e, tau=tau: e.dma_start(
                out=ybin[tau].ap().rearrange("(db p) t -> p db t", p=128),
                in_=a32[:, o_y:o_y + DBM * TC].rearrange("p (db t) -> p db t", db=DBM)),
                waits=[yadd_last], inc=f"Ly{tau}")
            ag_ev = P.add("pool", lambda e, tau=tau: e.collective_compute(
                "AllGather", ALU.bypass, replica_groups=PAIRS,
                ins=[ybin[tau].ap().opt()],
                outs=[ybag.ap()[tau * D:(tau + 1) * D, :].opt()]),
                waits=[yb_ev], inc="Cc")
            ag_evs.append(ag_ev)
            prev_cb2 = None
            chunk_bar = [scan_ev, yadd_last, yb_ev, proj_ev, ev_dtr, ev_bc, last_evx]

        if PART == "dtw":
            prev = ag_evs[-1]
            P.add("sync", lambda e: e.nop(), waits=[prev])
        if PART == "mixer":
            prev = ag_evs[-1]
            for q in range(2 * HG):
                eq = P.add("pool", lambda e, q=q: e.dma_start(
                    out=f32v(o_yf + (q % HG) * TC, TC),
                    in_=ybag.ap()[q * 128:(q + 1) * 128, :]),
                    waits=[prev], inc="Gy")
                prev = P.add("sync", lambda e, q=q: e.dma_start(
                    out=out[(q % HG) * 128:((q % HG) + 1) * 128, (q // HG) * TC:(q // HG + 1) * TC],
                    in_=f32v(o_yf + (q % HG) * TC, TC)), waits=[eq], inc="Lo0")

        all_ag = ag_evs[-1] if ag_evs else None
        slab_bar = []
        for sl in (range(2) if PART == "all" else ()):
            def yf_load(e, k=sl * 16):
                v = e.value_load(offs_s[0:1, k:k + 1])
                src = ybag.ap()[bass.ds(v, D), :].rearrange("(hg p) t -> p hg t", p=128)
                return e.dma_start(
                    out=a32[:, o_yf:o_yf + HG * TC].rearrange("p (hg t) -> p hg t", hg=HG),
                    in_=src)
            ev_yf = P.add("pool", yf_load,
                          waits=[all_ag, ev_flush] + slab_bar, inc="Gy")
            yf_evs = [ev_yf] * HG

            def emit_ln_stats(src_v, pre_evs, extra0):
                sw = [None, None]
                ev2 = None
                for hg in range(HG):
                    evq = P.add("act", lambda e, hg=hg: e.activation(
                        sqv(hg % 2), src_v(hg), AF.Square),
                        waits=[pre_evs[hg], sw[hg % 2]] + (extra0 if hg < 2 else []), inc="Ac")
                    P.add("pe", lambda e, hg=hg: e.matmul(
                        pb[0][0:1, :], one128_s[:], src_v(hg),
                        start=(hg == 0), stop=(hg == HG - 1)),
                        waits=[pre_evs[hg], ev_flush] + (extra0 if hg == 0 else []))
                    ev2 = P.add("pe", lambda e, hg=hg: e.matmul(
                        pb[1][0:1, :], one128_s[:], sqv(hg % 2),
                        start=(hg == 0), stop=(hg == HG - 1)),
                        waits=[evq], inc="Pe")
                    sw[hg % 2] = ev2
                return ev2, ev2

            def emit_ln_rows(stat_ev):
                P.add("dve", lambda e: e.tensor_scalar_mul(row(0), pb[0][0:1, :], INV_D),
                      waits=[stat_ev])
                P.add("dve", lambda e: e.tensor_scalar_mul(row(3), pb[1][0:1, :], INV_D))
                P.add("dve", lambda e: e.tensor_mul(row(2), row(0), row(0)))
                P.add("dve", lambda e: e.tensor_sub(row(3), row(3), row(2)))
                ev_var = P.add("dve", lambda e: e.tensor_scalar_add(row(3), row(3), EPS_LN),
                               inc="Dv")
                ev_ln = P.add("act", lambda e: e.activation(row(2), row(3), AF.Ln),
                              waits=[ev_var], inc="Ac")
                ev_hf = P.add("dve", lambda e: e.tensor_scalar_mul(row(2), row(2), -0.5),
                              waits=[ev_ln], inc="Dv")
                ev_rstd = P.add("act", lambda e: e.activation(row(1), row(2), AF.Exp),
                                waits=[ev_hf], inc="Ac")
                ev_b = P.add("dve", lambda e: e.scalar_tensor_tensor(
                    row(2), row(0), -1.0, row(1), ALU.mult, ALU.mult),
                    waits=[ev_rstd], inc="Dv")
                eva_r = P.add("pe", lambda e: e.matmul(pb[2][:, :], onec_s[:], row(1),
                                                       start=True, stop=True),
                              waits=[ev_b, ev_flush], inc="Pe")
                evb_r = P.add("pe", lambda e: e.matmul(pb[3][:, :], onec_s[:], row(2),
                                                       start=True, stop=True), inc="Pe")
                return ev_b, eva_r, evb_r

            st1, st2 = emit_ln_stats(yfv, yf_evs, slab_bar)
            ev_b, eva_r, evb_r = emit_ln_rows(st2)
            x1_evs = []
            for hg in range(HG):
                P.add("dve", lambda e, hg=hg: e.tensor_mul(yfv(hg), yfv(hg), pb[2][:, :]),
                      waits=[evb_r])
                P.add("dve", lambda e, hg=hg: e.tensor_add(yfv(hg), yfv(hg), pb[3][:, :]))
                x1_evs.append(P.add("dve", lambda e, hg=hg: e.tensor_copy(x1bv(hg), yfv(hg)),
                                    inc="Dv"))
            wgu_war = [None, None]
            gu_done = [None, None]
            ft_war = [None, None]
            gt_war = [None, None]
            fdma_evs = []
            for it in range(NIT):
                ii = it % 2
                evw = P.add("pool", lambda e, it=it, ii=ii: e.dma_start(
                    out=b16v(o_wgu[ii], 4096), in_=hgu[it]), waits=[wgu_war[ii]], inc=f"Gw{ii}")
                ba, bb = 4 + 2 * ii, 5 + 2 * ii
                evg = evu = None
                for kg in range(DBA):
                    st, sp = kg == 0, kg == DBA - 1
                    evg = P.add("pe", lambda e, kg=kg, st=st, sp=sp, ba=ba, ii=ii: e.matmul(
                        pb[ba][:, :], wguv(ii, kg * 128, 128), x1bv(kg), start=st, stop=sp),
                        waits=[evw, x1_evs[kg]] + ([gu_done[ii]] if st else []),
                        inc="Pe" if sp else None) or evg
                    evu = P.add("pe", lambda e, kg=kg, st=st, sp=sp, bb=bb, ii=ii: e.matmul(
                        pb[bb][:, :], wguv(ii, 2048 + kg * 128, 128), x1bv(kg),
                        start=st, stop=sp), inc="Pe" if sp else None) or evu
                wgu_war[ii] = evu
                evgt = P.add("act", lambda e, it=it, ii=ii, ba=ba: e.activation(
                    gtv(ii), pb[ba][:, :], AF.Silu, scale=sg_s[:, it:it + 1]),
                    waits=[evg, evu, gt_war[ii]], inc="Ac")
                evft = P.add("dve", lambda e, it=it, ii=ii, bb=bb: e.scalar_tensor_tensor(
                    ftv(ii), pb[bb][:, :], su_s[:, it:it + 1], gtv(ii), ALU.mult, ALU.mult),
                    waits=[evgt, ft_war[ii]], inc="Dv")
                gu_done[ii] = evft
                gt_war[ii] = evft
                evfd = P.add("sync", lambda e, it=it, ii=ii: e.dma_start(
                    out=fuse_d[it], in_=ftv(ii)), waits=[evft], inc=f"Lf{ii}")
                fdma_evs.append(evfd)
                ft_war[ii] = evfd
            ev_evac_all = [None] * HG
            prev_evac = []
            wd_war = [None] * 4
            frd_war = [None] * 4
            for hh in range(2):
                down_done = [None] * 8
                evd = None
                for it in range(NIT):
                    ii = it % 4
                    evwd = P.add("pool", lambda e, it=it, ii=ii, hh=hh: e.dma_start(
                        out=b16v(o_wd[ii], 1024), in_=hd[it, :, hh * 1024:(hh + 1) * 1024]),
                        waits=[wd_war[ii]], inc=f"Gv{ii}")
                    evfr = P.add("sync", lambda e, it=it, ii=ii: e.dma_start(
                        out=frdv(ii), in_=fuse_d[it]),
                        waits=[fdma_evs[-2], fdma_evs[-1], frd_war[ii]], inc=f"Lr{ii}")
                    for g8 in range(8):
                        st, sp = it == 0, it == NIT - 1
                        w0 = [evwd, evfr]
                        if st:
                            w0 = w0 + [ev_b, x1_evs[-1]] + prev_evac
                            if hh == 0:
                                w0 = w0 + [gu_done[0], gu_done[1]]
                        evd = P.add("pe", lambda e, ii=ii, g8=g8, st=st, sp=sp: e.matmul(
                            pb[g8][:, :], wdv(ii, g8 * 128, 128), frdv(ii),
                            start=st, stop=sp), waits=w0,
                            inc="Pe" if (g8 == 7 or sp) else None) or evd
                        if sp:
                            down_done[g8] = evd
                    wd_war[ii] = evd
                    frd_war[ii] = evd
                cur = []
                for g8 in range(8):
                    hg = hh * 8 + g8
                    eve = P.add("dve", lambda e, g8=g8, hg=hg: e.scalar_tensor_tensor(
                        y2v(hg), pb[g8][:, :], sd_s[:, hg:hg + 1], yfv(hg),
                        ALU.mult, ALU.add), waits=[down_done[g8]], inc="Dv")
                    cur.append(eve)
                    ev_evac_all[hg] = eve
                prev_evac = cur
            st1, st2 = emit_ln_stats(y2v, ev_evac_all, ev_evac_all)
            ev_b2, eva_r2, evb_r2 = emit_ln_rows(st2)
            ost_war = [None, None]
            slab_bar = []
            for hg in range(HG):
                P.add("dve", lambda e, hg=hg: e.tensor_mul(ostv(hg % 2), y2v(hg), pb[2][:, :]),
                      waits=[evb_r2, ost_war[hg % 2]])
                evo = P.add("dve", lambda e, hg=hg: e.tensor_add(
                    ostv(hg % 2), ostv(hg % 2), pb[3][:, :]), inc="Dv")
                evod = P.add("sync", lambda e, hg=hg, sl=sl: e.dma_start(
                    out=out[hg * 128:(hg + 1) * 128, sl * TC:(sl + 1) * TC], in_=ostv(hg % 2)),
                    waits=[evo], inc=f"Lo{hg % 2}")
                ost_war[hg % 2] = evod
                slab_bar.append(evod)
            slab_bar.append(ev_evac_all[-1])
            slab_bar.append(st2)

        P.add("sync", lambda e: e.nop(),
              waits=[(s, P.cnt[s]) for s in list(P.cnt) if is_dma_sem(s)])

        with nc.Block() as block:
            @block.sync
            def _(eng):
                P.emit_engine("sync", eng, sems)

            @block.tensor
            def _(eng):
                P.emit_engine("pe", eng, sems)

            @block.scalar
            def _(eng):
                P.emit_engine("act", eng, sems)

            @block.vector
            def _(eng):
                P.emit_engine("dve", eng, sems)

            @block.gpsimd
            def _(eng):
                P.emit_engine("pool", eng, sems)

    return nc


_NC_CACHE = {}


def _get_nc():
    if "nc" not in _NC_CACHE:
        _NC_CACHE["nc"] = build_nc()
    return _NC_CACHE["nc"]


def kernel(L0_attn_x, L0_attn_conv_w, L0_attn_conv_b, L0_attn_x_proj_w,
           L0_attn_dt_proj_w, L0_attn_dt_proj_b, L0_attn_A_log, L0_attn_D_param,
           L0_ffn_w_gate, L0_ffn_w_up, L0_ffn_w_down, L0_ffn_s_gate,
           L0_ffn_s_up, L0_ffn_s_down):
    x = np.asarray(L0_attn_x, np.float32)
    conv_w = np.asarray(L0_attn_conv_w, np.float32)
    conv_b = np.asarray(L0_attn_conv_b, np.float32)
    x_proj_w = np.asarray(L0_attn_x_proj_w, np.float32)
    dt_proj_w = np.asarray(L0_attn_dt_proj_w, np.float32)
    dt_proj_b = np.asarray(L0_attn_dt_proj_b, np.float32)
    A_log = np.asarray(L0_attn_A_log, np.float32)
    D_param = np.asarray(L0_attn_D_param, np.float32)
    wg8 = np.asarray(L0_ffn_w_gate)
    wu8 = np.asarray(L0_ffn_w_up)
    wd8 = np.asarray(L0_ffn_w_down)
    s_gate = np.asarray(L0_ffn_s_gate, np.float32)
    s_up = np.asarray(L0_ffn_s_up, np.float32)
    s_down = np.asarray(L0_ffn_s_down, np.float32)

    def half_prep(role):
        lo = role * 1024
        perm = np.concatenate([np.arange(lo, lo + 1024),
                               np.arange((1 - role) * 1024, (1 - role) * 1024 + 1024)])
        d = {}
        d["cw"] = np.ascontiguousarray(
            conv_w[perm, 0, :].reshape(16, 128, 4).transpose(1, 0, 2).reshape(128, 64))
        d["cbi"] = np.ascontiguousarray(conv_b[perm].reshape(16, 128).T)
        d["dpi"] = np.ascontiguousarray(D_param[lo:lo + 1024].reshape(8, 128).T)
        d["dtbi"] = np.ascontiguousarray(dt_proj_b[lo:lo + 1024].reshape(8, 128).T)
        d["alog"] = np.ascontiguousarray(
            A_log[lo:lo + 1024].reshape(8, 128, 16).transpose(1, 0, 2).reshape(128, 128))
        d["wxp"] = np.ascontiguousarray(x_proj_w.T[perm])
        d["wdt"] = np.ascontiguousarray(dt_proj_w[lo:lo + 1024].T)
        off = np.zeros((1, 32), np.uint32)
        for sl in range(2):
            for hg in range(HG):
                off[0, sl * 16 + hg] = (role * 2 + sl) * D + hg * 128
        d["offs"] = off
        return d, perm

    halves = [half_prep(0), half_prep(1)]

    esel = np.zeros((32, 2048), np.float32)
    esel2 = np.zeros((32, 2048), np.float32)
    for s in range(16):
        esel[s, s * 128:(s + 1) * 128] = 1.0
        esel2[16 + s, s * 128:(s + 1) * 128] = 1.0
    onescol = np.ones((1, 128), np.float32)
    ones128 = np.ones((128, 1), np.float32)
    g4 = wg8.reshape(16, 128, 64, 128).transpose(2, 1, 0, 3)
    u4 = wu8.reshape(16, 128, 64, 128).transpose(2, 1, 0, 3)
    hgu = np.ascontiguousarray(np.stack([g4, u4], axis=2).reshape(NIT, 128, 4096))
    hd = np.ascontiguousarray(wd8.reshape(NIT, 128, 2048))
    sgl = np.ascontiguousarray(s_gate.reshape(64, 128).T)
    sul = np.ascontiguousarray(s_up.reshape(64, 128).T)
    sdl = np.ascontiguousarray(s_down.reshape(16, 128).T)

    in_maps = []
    for c in range(8):
        b, role = c // 2, c % 2
        d, perm = halves[role]
        xTh = np.zeros((D, T + 3), np.float32)
        xTh[:, 3:] = x[b].T[perm]
        m = {"xT": xTh, "esel": esel, "esel2": esel2, "onescol": onescol, "ones128": ones128,
             "hgu": hgu, "hd": hd, "sgd": sgl, "sud": sul, "sdd": sdl}
        m.update(d)
        in_maps.append(m)

    nc = _get_nc()
    res = run_bass_kernel_spmd(nc, in_maps, core_ids=list(range(8)))

    result = np.empty((BATCH, T, D), np.float32)
    for c in range(8):
        b, role = c // 2, c % 2
        o = np.asarray(res.results[c]["out"])
        result[b, role * 1024:(role + 1) * 1024, :] = o.T
    return result


